# version 01
# baseline (speedup 1.0000x reference)
import contextlib
import numpy as np
import concourse.bass as bass
import concourse.mybir as mybir

F32 = mybir.dt.float32
BF16 = mybir.dt.bfloat16
I32 = mybir.dt.int32
AF = mybir.ActivationFunctionType
ALU = mybir.AluOpType
AX = mybir.AxisListType

ENGINES = ("sync", "tensor", "vector", "scalar", "gpsimd")


class Sem:
    __slots__ = ("name", "count", "h")

    def __init__(self, name, h):
        self.name = name
        self.count = 0
        self.h = h


class Prog:

    def __init__(self, nc, es):
        self.nc = nc
        self.es = es
        self.ops = []

    def sem(self, name):
        return Sem(name, self.es.enter_context(self.nc.semaphore(name)))

    def sbuf(self, name, shape, dtype):
        return self.es.enter_context(self.nc.sbuf_tensor(name, shape, dtype))

    def psum(self, name, shape, dtype=F32):
        return self.es.enter_context(self.nc.psum_tensor(name, shape, dtype))

    def op(self, engine, fn, waits=(), inc=None, incs=None):
        assert engine in ENGINES, engine
        for s, v in waits:
            assert v <= s.count, (s.name, v, s.count, "wait on future sem value")
        ilist = list(incs) if incs else ([] if inc is None else [inc])
        for s, n in ilist:
            s.count += n
        self.ops.append((engine, tuple(waits), fn, ilist))

    def dma(self, engine, out, in_, waits=(), inc=None):
        def fn(eng, out=out, in_=in_):
            return eng.dma_start(out=out, in_=in_)
        self.op(engine, fn, waits=waits, inc=inc)

    def dma_ch(self, engine, out, in_, sem, waits=()):
        def fn(eng, out=out, in_=in_):
            return eng.dma_start(out=out, in_=in_)
        self.op(engine, fn, waits=waits, inc=(sem, 16))

    def build(self):
        with self.nc.Block() as block:
            for name in ENGINES:
                ops = [o for o in self.ops if o[0] == name]
                if not ops:
                    continue

                def make(ops):
                    def emit(eng):
                        for _, waits, fn, ilist in ops:
                            for s, v in waits:
                                eng.wait_ge(s.h, v)
                            ins = fn(eng)
                            for s, n in ilist:
                                ins = ins.then_inc(s.h, n)
                    return emit

                getattr(block, name)(make(ops))


def ceil_div(a, b):
    return -(-a // b)



RSQRT_HD = 0.08838834764831845
EPS_LN = 1e-5
EPS_RMS = 1.1920929e-07


class L1:
    def __init__(self, T=1024, TT=512, D=2048, H=8192, NH=16, NKV=4):
        self.__dict__.update(T=T, TT=TT, D=D, H=H, NH=NH, NKV=NKV)
        self.CD, self.CH, self.NT = D // 128, H // 128, T // TT
        self.O3 = (NH + 2 * NKV) * 128

    def build(self):
        T, TT, D, H, NH, NKV = self.T, self.TT, self.D, self.H, self.NH, self.NKV
        CD, O3 = self.CD, self.O3
        nc = bass.Bass("TRN2", target_bir_lowering=False, debug=False,
                       detect_race_conditions=False)

        dr = {}
        dr["x_fm"] = nc.dram_tensor("x_fm", [D, T], F32, kind="ExternalInput").ap()
        dr["wgT"] = nc.dram_tensor("wgT", [D, H], F32, kind="ExternalInput").ap()
        dr["wuT"] = nc.dram_tensor("wuT", [D, H], F32, kind="ExternalInput").ap()
        dr["wdT"] = nc.dram_tensor("wdT", [H, D], F32, kind="ExternalInput").ap()
        dr["wqkvT"] = nc.dram_tensor("wqkvT", [D, O3], F32, kind="ExternalInput").ap()
        dr["cos_t"] = nc.dram_tensor("cos_t", [128, T], F32, kind="ExternalInput").ap()
        dr["sin_t"] = nc.dram_tensor("sin_t", [128, T], F32, kind="ExternalInput").ap()
        dr["rotc_in"] = nc.dram_tensor("rotc_in", [128, 128], F32, kind="ExternalInput").ap()
        dr["gain_in"] = nc.dram_tensor("gain_in", [1, NH], F32, kind="ExternalInput").ap()
        dr["eye_in"] = nc.dram_tensor("eye_in", [128, 128], F32, kind="ExternalInput").ap()
        dr["sel16_in"] = nc.dram_tensor("sel16_in", [16, 16 * 128], F32, kind="ExternalInput").ap()
        dr["selc_in"] = nc.dram_tensor("selc_in", [128, 2 * CD - 1], BF16, kind="ExternalInput").ap()

        dr["x1_out"] = nc.dram_tensor("x1_out", [D, T], F32, kind="ExternalOutput").ap()
        dr["q_out"] = nc.dram_tensor("q_out", [NH, 128, T], F32, kind="ExternalOutput").ap()
        dr["k_out"] = nc.dram_tensor("k_out", [NKV, 128, T], F32, kind="ExternalOutput").ap()
        dr["v_out"] = nc.dram_tensor("v_out", [NKV, T, 128], F32, kind="ExternalOutput").ap()
        if getattr(self, "debug_weff", False):
            dr["dbg_weff"] = nc.dram_tensor("dbg_weff", [self.O3 // 256, 128, 16 * 256], F32,
                                            kind="ExternalOutput").ap()
            dr["dbg_wb"] = nc.dram_tensor("dbg_wb", [self.O3 // 256, 128, 16 * 256], F32,
                                          kind="ExternalOutput").ap()

        with contextlib.ExitStack() as es:
            p = Prog(nc, es)
            self._body(nc, p, dr)
            p.build()
        return nc

    def _body(self, nc, p, dr):
        T, TT, D, H, NH, NKV = self.T, self.TT, self.D, self.H, self.NH, self.NKV
        CD, CH, NT, O3 = self.CD, self.CH, self.NT, self.O3
        SLAB = 256
        NSLAB = O3 // SLAB

        xT = p.sbuf("xT", [128, CD * T], F32)
        h1 = p.sbuf("h1", [128, CD * TT], F32)
        act = p.sbuf("act", [128, 2 * TT], F32)
        y_acc = p.sbuf("y_acc", [128, CD * TT], F32)
        wg_s = p.sbuf("wg_s", [128, 2 * D], F32)
        wu_s = p.sbuf("wu_s", [128, 2 * D], F32)
        wd_s = p.sbuf("wd_s", [128, 2 * D], F32)
        cos2 = p.sbuf("cos2", [128, T], F32)
        sin2 = p.sbuf("sin2", [128, T], F32)
        rotc = p.sbuf("rotc", [128, 128], F32)
        sq = p.sbuf("sq", [128, 2 * TT], F32)
        stats = p.sbuf("stats", [128, 3 * TT], F32)
        consts = p.sbuf("consts", [128, 144], F32)
        gain_s = p.sbuf("gain_s", [1, NH], F32)
        selcol = p.sbuf("selcol", [128, 2 * CD - 1], BF16)
        eye_s = p.sbuf("eye_s", [128, 128], F32)

        wbb = y_acc[:, 0:CD * SLAB // 2].bitcast(BF16)
        wab = y_acc[:, 2048:4096].bitcast(BF16)
        rb = y_acc[:, 4096:4224].bitcast(BF16)
        g1 = y_acc[:, 4352:4608]
        g2 = y_acc[:, 4608:4864]
        qraw = y_acc[:, 4864:5376]
        ro = y_acc[:, 5376:5888]
        scl_s = p.sbuf("scl_s", [16, 896], F32)
        scl16 = scl_s[:, 0:256]
        scf16 = scl_s[:, 256:512]
        inv16 = scl_s[:, 512:768]
        sclb = scl_s[:, 768:896].bitcast(BF16)
        sel16 = y_acc[0:16, 5888:7936]

        ps = [p.psum(f"ps{i}", [128, 512], F32) for i in range(8)]
        PS_GATE, PS_UP, PS_D0, PS_D1, PS_S0, PS_S1, PS_B0, PS_B1 = range(8)

        s_in = p.sem("s_in")
        s_gu = [p.sem("s_gu0"), p.sem("s_gu1")]
        s_wd = [p.sem("s_wd0"), p.sem("s_wd1")]
        s_st = [p.sem("s_st0"), p.sem("s_st1")]
        s_x1 = p.sem("s_x1")
        s_out = p.sem("s_out")
        s_dbg = p.sem("s_dbg")
        mm = p.sem("mms")
        vv = p.sem("vvs")
        aa = p.sem("aas")

        self._sqpar = 0
        self._sq_marks = {}
        self._free = {}
        self._weff_dump = {}

        p.dma_ch("sync", xT[:, :].rearrange("p (c t) -> p c t", c=CD),
                 dr["x_fm"].rearrange("(c p) t -> p c t", p=128), s_in)
        p.dma_ch("sync", cos2[:, :], dr["cos_t"], s_in)
        p.dma_ch("sync", sin2[:, :], dr["sin_t"], s_in)
        p.dma_ch("sync", rotc[:, :], dr["rotc_in"], s_in)
        p.dma_ch("sync", gain_s[:, :], dr["gain_in"], s_in)
        p.dma_ch("sync", eye_s[:, :], dr["eye_in"], s_in)
        p.dma_ch("sync", selcol[:, :], dr["selc_in"], s_in)
        lds0 = s_in.count
        p.op("vector", lambda eng: eng.memset(consts[:, 0:1], 1.0), inc=(vv, 1))
        p.op("vector", lambda eng: eng.memset(consts[0:1, 1:129], 1.0), inc=(vv, 1))
        consts_ready = vv.count
        ones_col = consts[:, 0:1]
        ones_row = consts[0:1, 1:129]

        A = stats[0:1, 0:TT]
        B = stats[0:1, TT:2 * TT]
        C = stats[0:1, 2 * TT:3 * TT]

        def rstd_from_B(extra_eps, width):
            p.op("vector", lambda eng: eng.tensor_scalar_add(
                B[:, 0:width], B[:, 0:width], extra_eps), inc=(vv, 1))
            p.op("vector", lambda eng: eng.reciprocal(
                C[:, 0:width], B[:, 0:width]), inc=(vv, 1))
            p.op("scalar", lambda eng: eng.activation(
                C[:, 0:width], C[:, 0:width], AF.Sqrt),
                waits=[(vv, vv.count)], inc=(aa, 1))
            p.op("scalar", lambda eng: eng.activation(
                A[:, 0:width], C[:, 0:width], AF.Square), inc=(aa, 1))
            p.op("vector", lambda eng: eng.tensor_mul(
                A[:, 0:width], A[:, 0:width], B[:, 0:width]),
                waits=[(aa, aa.count)], inc=(vv, 1))
            p.op("vector", lambda eng: eng.tensor_scalar(
                A[:, 0:width], A[:, 0:width], -0.5, 1.5, ALU.mult, ALU.add),
                inc=(vv, 1))
            p.op("vector", lambda eng: eng.tensor_mul(
                C[:, 0:width], C[:, 0:width], A[:, 0:width]), inc=(vv, 1))
            return vv.count

        def ln(src_view, dst_view, eps, wait0=(), n_chunks=CD, width=TT):
            ndim = n_chunks * 128
            w0 = list(wait0) + [(vv, consts_ready)]
            for c in range(n_chunks):
                par = self._sqpar
                self._sqpar ^= 1
                wsq = [(mm, self._sq_marks[par])] if par in self._sq_marks else []
                p.op("scalar", lambda eng, c=c, par=par: eng.activation(
                    sq[:, par * TT:par * TT + width], src_view(c), AF.Square),
                    waits=wsq + (w0 if c == 0 else []), inc=(aa, 1))
                p.op("tensor", lambda eng, c=c: eng.matmul(
                    ps[PS_S0][0:1, 0:width], ones_col, src_view(c),
                    start=(c == 0), stop=(c == n_chunks - 1)),
                    waits=(w0 + self._free.get(PS_S0, [])) if c == 0 else (),
                    inc=(mm, 1))
                p.op("tensor", lambda eng, c=c, par=par: eng.matmul(
                    ps[PS_S1][0:1, 0:width], ones_col,
                    sq[:, par * TT:par * TT + width],
                    start=(c == 0), stop=(c == n_chunks - 1)),
                    waits=[(aa, aa.count)] + (self._free.get(PS_S1, []) if c == 0 else []),
                    inc=(mm, 1))
                self._sq_marks[par] = mm.count
            stats_mm = mm.count
            w = [(mm, stats_mm)]
            p.op("vector", lambda eng: eng.tensor_scalar_mul(
                A[:, 0:width], ps[PS_S0][0:1, 0:width], 1.0 / ndim),
                waits=w, inc=(vv, 1))
            self._free[PS_S0] = [(vv, vv.count)]
            p.op("vector", lambda eng: eng.tensor_scalar_mul(
                B[:, 0:width], ps[PS_S1][0:1, 0:width], 1.0 / ndim),
                waits=w, inc=(vv, 1))
            self._free[PS_S1] = [(vv, vv.count)]
            mean_done = vv.count
            p.op("tensor", lambda eng: eng.matmul(
                ps[PS_B0][:, 0:width], ones_row, A[:, 0:width],
                start=True, stop=True),
                waits=[(vv, mean_done)] + self._free.get(PS_B0, []), inc=(mm, 1))
            mbc = mm.count
            p.op("scalar", lambda eng: eng.activation(
                C[:, 0:width], A[:, 0:width], AF.Square),
                waits=[(vv, mean_done)], inc=(aa, 1))
            p.op("vector", lambda eng: eng.tensor_sub(
                B[:, 0:width], B[:, 0:width], C[:, 0:width]),
                waits=[(aa, aa.count), (mm, mbc)], inc=(vv, 1))
            rstd_done = rstd_from_B(eps, width)
            p.op("tensor", lambda eng: eng.matmul(
                ps[PS_B1][:, 0:width], ones_row, C[:, 0:width],
                start=True, stop=True),
                waits=[(vv, rstd_done)] + self._free.get(PS_B1, []), inc=(mm, 1))
            bca = mm.count
            for c in range(n_chunks):
                p.op("vector", lambda eng, c=c: eng.tensor_sub(
                    dst_view(c), src_view(c), ps[PS_B0][:, 0:width]),
                    waits=[(mm, bca)] if c == 0 else (), inc=(vv, 1))
                p.op("vector", lambda eng, c=c: eng.tensor_mul(
                    dst_view(c), dst_view(c), ps[PS_B1][:, 0:width]), inc=(vv, 1))
            self._free[PS_B0] = [(vv, vv.count)]
            self._free[PS_B1] = [(vv, vv.count)]
            return vv.count, bca

        def xv(tile):
            return lambda c: xT[:, c * T + tile * TT: c * T + (tile + 1) * TT]

        def h1v(c):
            return h1[:, c * TT:(c + 1) * TT]

        gu_marks = {}
        wd_marks = {}
        gate_free = aa.count
        up_free = vv.count
        dn_free = {}
        x1dma = []

        def load_gu(tile, h):
            gseq = tile * CH + h
            w = [(mm, gu_marks[gseq - 2])] if gseq - 2 in gu_marks else []
            buf = gseq % 2
            sem = s_gu[buf]
            p.dma_ch("sync", wg_s[:, buf * D:(buf + 1) * D].rearrange("p (c n) -> p c n", c=CD),
                     dr["wgT"][:, h * 128:(h + 1) * 128].rearrange("(c p) n -> p c n", p=128),
                     sem, waits=w)
            p.dma_ch("sync", wu_s[:, buf * D:(buf + 1) * D].rearrange("p (c n) -> p c n", c=CD),
                     dr["wuT"][:, h * 128:(h + 1) * 128].rearrange("(c p) n -> p c n", p=128),
                     sem)
            return (sem, sem.count)

        def load_wd(tile, g):
            gseq = tile * (CH // 2) + g
            w = [(mm, wd_marks[gseq - 1])] if gseq - 1 in wd_marks else []
            sem = s_wd[gseq % 2]
            for i in range(2):
                h = 2 * g + i
                p.dma_ch("sync",
                         wd_s[:, i * D:(i + 1) * D].rearrange("p (c n) -> p c n", c=CD),
                         dr["wdT"][h * 128:(h + 1) * 128, :].rearrange("p (c n) -> p c n", c=CD),
                         sem, waits=w if i == 0 else ())
            return (sem, sem.count)

        for tile in range(NT):
            h1_ready, _ = ln(xv(tile), h1v, EPS_LN,
                             wait0=[(s_in, lds0)] if tile == 0 else ())
            gu_ld = {0: load_gu(tile, 0), 1: load_gu(tile, 1)}
            wd_ld = {0: load_wd(tile, 0)}
            for h in range(CH):
                gseq = tile * CH + h
                buf = gseq % 2
                gi = h % 2
                g = h // 2
                for k in range(CD):
                    p.op("tensor", lambda eng, k=k, buf=buf: eng.matmul(
                        ps[PS_GATE][:, 0:TT],
                        wg_s[:, buf * D + k * 128: buf * D + (k + 1) * 128],
                        h1v(k), start=(k == 0), stop=(k == CD - 1)),
                        waits=[gu_ld[h], (vv, h1_ready), (aa, gate_free)]
                        if k == 0 else (), inc=(mm, 1) if k == CD - 1 else None)
                gate_mm = mm.count
                for k in range(CD):
                    p.op("tensor", lambda eng, k=k, buf=buf: eng.matmul(
                        ps[PS_UP][:, 0:TT],
                        wu_s[:, buf * D + k * 128: buf * D + (k + 1) * 128],
                        h1v(k), start=(k == 0), stop=(k == CD - 1)),
                        waits=[(vv, up_free)] if k == 0 else (),
                        inc=(mm, 1) if k == CD - 1 else None)
                gu_marks[gseq] = mm.count
                up_mm = mm.count
                if h + 2 < CH:
                    gu_ld[h + 2] = load_gu(tile, h + 2)
                p.op("scalar", lambda eng, gi=gi: eng.activation(
                    act[:, gi * TT:(gi + 1) * TT], ps[PS_GATE][:, 0:TT], AF.Silu),
                    waits=[(mm, gate_mm)], inc=(aa, 1))
                gate_free = aa.count
                p.op("vector", lambda eng, gi=gi: eng.tensor_mul(
                    act[:, gi * TT:(gi + 1) * TT], act[:, gi * TT:(gi + 1) * TT],
                    ps[PS_UP][:, 0:TT]),
                    waits=[(mm, up_mm), (aa, aa.count)], inc=(vv, 1))
                up_free = vv.count
                act_ready = vv.count
                if gi == 1:
                    gsq = tile * (CH // 2) + g
                    for o in range(CD):
                        bank = PS_D0 if o % 2 == 0 else PS_D1
                        wdn = list([(vv, dn_free[o - 2])] if o - 2 in dn_free else [])
                        if o == 0:
                            wdn += [(vv, act_ready), wd_ld[g]]
                        for i in range(2):
                            p.op("tensor", lambda eng, o=o, i=i, bank=bank: eng.matmul(
                                ps[bank][:, 0:TT],
                                wd_s[:, i * D + o * 128: i * D + (o + 1) * 128],
                                act[:, i * TT:(i + 1) * TT],
                                start=(i == 0), stop=(i == 1)),
                                waits=wdn if i == 0 else (),
                                inc=(mm, 1) if i == 1 else None)
                        dn_mm = mm.count
                        if g == 0:
                            p.op("vector", lambda eng, o=o, bank=bank: eng.tensor_copy(
                                y_acc[:, o * TT:(o + 1) * TT], ps[bank][:, 0:TT]),
                                waits=[(mm, dn_mm)], inc=(vv, 1))
                        else:
                            p.op("vector", lambda eng, o=o, bank=bank: eng.tensor_add(
                                y_acc[:, o * TT:(o + 1) * TT],
                                y_acc[:, o * TT:(o + 1) * TT], ps[bank][:, 0:TT]),
                                waits=[(mm, dn_mm)], inc=(vv, 1))
                        dn_free[o] = vv.count
                        self._free[bank] = [(vv, vv.count)]
                    wd_marks[gsq] = mm.count
                    if g + 1 < CH // 2:
                        wd_ld[g + 1] = load_wd(tile, g + 1)
            for o in range(CD):
                p.op("vector", lambda eng, o=o, tile=tile: eng.scalar_tensor_tensor(
                    xv(tile)(o), y_acc[:, o * TT:(o + 1) * TT], 0.5,
                    xv(tile)(o), ALU.mult, ALU.add), inc=(vv, 1))
            x1_done = vv.count
            p.dma_ch("gpsimd",
                     dr["x1_out"].rearrange("(c p) t -> p c t", p=128)[:, :, tile * TT:(tile + 1) * TT],
                     xT[:, :].rearrange("p (c t) -> p c t", c=CD)[:, :, tile * TT:(tile + 1) * TT],
                     s_x1, waits=[(vv, x1_done)])
            x1dma.append(s_x1.count)

        ha_ready = []
        for tile in range(NT):
            mark, _ = ln(xv(tile), xv(tile), EPS_LN, wait0=[(s_x1, x1dma[-1])])
            ha_ready.append(mark)

        stage_bufs = (wg_s, wu_s)
        weff_bufs = (wd_s, h1)
        stage_ld = {}
        stage_read = {}
        weff_done = {}
        wb_read = {}

        def load_slab(s):
            w = []
            if s - 2 in stage_read:
                w.append((aa, stage_read[s - 2]))
            if s == 0:
                w.append((vv, ha_ready[-1]))
            sem = s_st[s % 2]
            p.dma_ch("sync",
                     stage_bufs[s % 2][:, 0:CD * SLAB].rearrange("p (c n) -> p c n", c=CD),
                     dr["wqkvT"][:, s * SLAB:(s + 1) * SLAB].rearrange("(c p) n -> p c n", p=128),
                     sem, waits=w)
            stage_ld[s] = (sem, sem.count)

        p.dma_ch("sync", sel16, dr["sel16_in"], s_in,
                 waits=[(s_x1, x1dma[-1])])
        sel16_ld = s_in.count
        load_slab(0)
        load_slab(1)

        for s in range(NSLAB):
            par = s % 2
            stage = stage_bufs[par]
            weff = weff_bufs[par]
            w = [stage_ld[s]]
            if s - 1 in wb_read:
                w.append((vv, wb_read[s - 1]))
            if s == 0:
                w.append((s_x1, x1dma[-1]))
            p.op("scalar", lambda eng, stage=stage: eng.copy(
                wbb[:, :], stage[:, 0:CD * SLAB]), waits=w, inc=(aa, 1))
            p.op("scalar", lambda eng, stage=stage: eng.activation(
                wab[:, :], stage[:, 0:CD * SLAB], AF.Abs), inc=(aa, 1))
            stage_read[s] = aa.count
            if s + 2 < NSLAB:
                load_slab(s + 2)
            if getattr(self, "debug_weff", False):
                p.op("vector", lambda eng, stage=stage: eng.tensor_copy(
                    y_acc[:, 7936:8192], stage[:, 0:256]), waits=[(aa, aa.count)], inc=(vv, 1))
                p.dma_ch("gpsimd", dr["dbg_wb"][s][:, 0:256], y_acc[:, 7936:8192],
                         s_dbg, waits=[(vv, vv.count)])
                self._weff_dump[s * 100 + 50] = s_dbg.count
            wab_ready = vv.count
            for c in range(CD):
                p.op("tensor", lambda eng, c=c: eng.matmul(
                    ps[PS_S0][0:16, 0:SLAB], selcol[:, (CD - 1 - c):(2 * CD - 1 - c)],
                    wab[:, c * SLAB:(c + 1) * SLAB],
                    start=(c == 0), stop=(c == CD - 1)),
                    waits=([(aa, stage_read[s])] + self._free.get(PS_S0, []))
                    if c == 0 else (), inc=(mm, 1))
            gsum_mm = mm.count
            p.op("vector", lambda eng: eng.tensor_scalar_mul(
                scl16, ps[PS_S0][0:16, 0:SLAB], 1.0 / 128.0),
                waits=[(mm, gsum_mm)], inc=(vv, 1))
            self._free[PS_S0] = [(vv, vv.count)]
            p.op("vector", lambda eng: eng.tensor_copy(sclb, scl16), inc=(vv, 1))
            p.op("vector", lambda eng: eng.tensor_copy(scf16, sclb), inc=(vv, 1))
            p.op("vector", lambda eng: eng.tensor_scalar_max(
                scf16, scf16, 1e-8), inc=(vv, 1))
            p.op("vector", lambda eng: eng.reciprocal(inv16, scf16), inc=(vv, 1))
            scl_done = vv.count
            for c in range(CD):
                p.op("tensor", lambda eng, c=c: eng.matmul(
                    ps[PS_B0][:, 0:SLAB], sel16[:, c * 128:(c + 1) * 128],
                    inv16, start=True, stop=True),
                    waits=([(vv, scl_done), (s_in, sel16_ld)] if c == 0 else [])
                    + self._free.get(PS_B0, []), inc=(mm, 1))
                p.op("tensor", lambda eng, c=c: eng.matmul(
                    ps[PS_B1][:, 0:SLAB], sel16[:, c * 128:(c + 1) * 128],
                    scf16, start=True, stop=True),
                    waits=self._free.get(PS_B1, []), inc=(mm, 1))
                bc = mm.count
                p.op("vector", lambda eng, c=c: eng.tensor_mul(
                    rb, wbb[:, c * SLAB:(c + 1) * SLAB], ps[PS_B0][:, 0:SLAB]),
                    waits=[(mm, bc)], inc=(vv, 1))
                self._free[PS_B0] = [(vv, vv.count)]
                p.op("vector", lambda eng: eng.tensor_single_scalar(
                    g1, rb, 0.5, ALU.is_gt), inc=(vv, 1))
                p.op("vector", lambda eng: eng.tensor_single_scalar(
                    g2, rb, -0.5, ALU.is_lt), inc=(vv, 1))
                p.op("vector", lambda eng: eng.tensor_sub(g1, g1, g2), inc=(vv, 1))
                wwf = []
                if c == 0 and s - 2 in weff_done:
                    wwf.append((mm, weff_done[s - 2]))
                    if self._weff_dump:
                        wwf.append((s_dbg, max(self._weff_dump.values())))
                p.op("vector", lambda eng, c=c, weff=weff: eng.tensor_mul(
                    weff[:, c * SLAB:(c + 1) * SLAB], g1, ps[PS_B1][:, 0:SLAB]),
                    waits=wwf, inc=(vv, 1))
                self._free[PS_B1] = [(vv, vv.count)]
            wb_read[s] = vv.count
            weff_ready = vv.count
            if getattr(self, "debug_weff", False):
                p.dma_ch("gpsimd", dr["dbg_weff"][s], weff[:, 0:CD * SLAB],
                         s_dbg, waits=[(vv, weff_ready)])
                self._weff_dump[s] = s_dbg.count

            for tile in range(NT):
                for oc in range(2):
                    bank = PS_GATE if oc == 0 else PS_UP
                    og = s * SLAB // 128 + oc
                    for k in range(CD):
                        p.op("tensor",
                             lambda eng, k=k, oc=oc, weff=weff, bank=bank, tile=tile:
                             eng.matmul(
                                 ps[bank][:, 0:TT],
                                 weff[:, k * SLAB + oc * 128: k * SLAB + oc * 128 + 128],
                                 xv(tile)(k), start=(k == 0), stop=(k == CD - 1)),
                             waits=[(vv, weff_ready), (vv, vv.count), (aa, aa.count)]
                             if k == 0 else (), inc=(mm, 1) if k == CD - 1 else None)
                    qkv_mm = mm.count
                    self._head(p, dr, og, tile, bank, qkv_mm, locals())
            weff_done[s] = mm.count

        p.op("gpsimd", lambda eng: eng.engine_nop(), waits=[(s_out, s_out.count), (s_x1, s_x1.count)])

    def _head(self, p, dr, og, tile, bank, qkv_mm, env):
        T, TT, NH, NKV = self.T, self.TT, self.NH, self.NKV
        ps = env["ps"]
        PS_D0, PS_D1, PS_S1, PS_B1 = 2, 3, 5, 7
        s_out, mm, vv, aa = (env[k] for k in ("s_out", "mm", "vv", "aa"))
        qraw, ro, sq = env["qraw"], env["ro"], env["sq"]
        cos2, sin2, rotc = env["cos2"], env["sin2"], env["rotc"]
        gain_s, eye_s = env["gain_s"], env["eye_s"]
        ones_col, ones_row = env["ones_col"], env["ones_row"]
        A, B, C = env["A"], env["B"], env["C"]
        s_in, lds0 = env["s_in"], env["lds0"]
        cosv = cos2[:, tile * TT:(tile + 1) * TT]
        sinv = sin2[:, tile * TT:(tile + 1) * TT]
        rstd_from_B = env["rstd_from_B"]

        p.op("vector", lambda eng: eng.tensor_copy(qraw, ps[bank][:, 0:TT]),
             waits=[(mm, qkv_mm)], inc=(vv, 1))
        evac = vv.count

        if og < NH:
            h, kind = og, "q"
        elif og < NH + NKV:
            h, kind = og - NH, "k"
        else:
            h, kind = og - NH - NKV, "v"

        if kind == "v":
            for st in range(TT // 128):
                p.op("tensor", lambda eng, st=st: eng.transpose(
                    ps[PS_D0][:, st * 128:(st + 1) * 128],
                    qraw[:, st * 128:(st + 1) * 128], eye_s[:, :]),
                    waits=([(vv, evac)] + self._free.get(PS_D0, []))
                    if st == 0 else (),
                    inc=(mm, 1) if st == TT // 128 - 1 else None)
            tmark = mm.count
            p.op("vector", lambda eng: eng.tensor_copy(ro, ps[PS_D0][:, 0:TT]),
                 waits=[(mm, tmark)], inc=(vv, 1))
            self._free[PS_D0] = [(vv, vv.count)]
            rmark = vv.count
            for st in range(TT // 128):
                p.dma_ch("gpsimd",
                         dr["v_out"][h][tile * TT + st * 128: tile * TT + (st + 1) * 128, :],
                         ro[:, st * 128:(st + 1) * 128],
                         s_out, waits=[(vv, rmark)] if st == 0 else ())
            return

        par = self._sqpar
        self._sqpar ^= 1
        wsq = [(mm, self._sq_marks[par])] if par in self._sq_marks else []
        p.op("scalar", lambda eng, par=par: eng.activation(
            sq[:, par * TT:(par + 1) * TT], qraw, AF.Square),
            waits=wsq + [(vv, evac)], inc=(aa, 1))
        p.op("tensor", lambda eng, par=par: eng.matmul(
            ps[PS_S1][0:1, 0:TT], ones_col, sq[:, par * TT:(par + 1) * TT],
            start=True, stop=True),
            waits=[(aa, aa.count)] + self._free.get(PS_S1, []), inc=(mm, 1))
        self._sq_marks[par] = mm.count
        smm = mm.count
        p.op("tensor", lambda eng: eng.matmul(
            ps[PS_D1][:, 0:TT], rotc[:, :], qraw, start=True, stop=True),
            waits=[(s_in, lds0), (vv, evac)] + self._free.get(PS_D1, []), inc=(mm, 1))
        rxm = mm.count
        p.op("vector", lambda eng: eng.tensor_scalar_mul(
            B, ps[PS_S1][0:1, 0:TT], 1.0 / 128.0),
            waits=[(mm, smm)], inc=(vv, 1))
        self._free[PS_S1] = [(vv, vv.count)]
        rstd_from_B(EPS_RMS, TT)
        if kind == "q":
            p.op("vector", lambda eng: eng.tensor_scalar_mul(
                C, C, RSQRT_HD), inc=(vv, 1))
            p.op("vector", lambda eng, h=h: eng.tensor_scalar_mul(
                C, C, gain_s[0:1, h:h + 1]), inc=(vv, 1))
        rstd_done = vv.count
        p.op("tensor", lambda eng: eng.matmul(
            ps[PS_B1][:, 0:TT], ones_row, C, start=True, stop=True),
            waits=[(vv, rstd_done)] + self._free.get(PS_B1, []), inc=(mm, 1))
        bmark = mm.count
        p.op("vector", lambda eng: eng.tensor_mul(ro, qraw, cosv), inc=(vv, 1))
        p.op("vector", lambda eng: eng.tensor_mul(qraw, ps[PS_D1][:, 0:TT], sinv),
             waits=[(mm, rxm)], inc=(vv, 1))
        self._free[PS_D1] = [(vv, vv.count)]
        p.op("vector", lambda eng: eng.tensor_add(ro, ro, qraw), inc=(vv, 1))
        p.op("vector", lambda eng: eng.tensor_mul(ro, ro, ps[PS_B1][:, 0:TT]),
             waits=[(mm, bmark)], inc=(vv, 1))
        self._free[PS_B1] = [(vv, vv.count)]
        omark = vv.count
        out = dr["q_out"] if kind == "q" else dr["k_out"]
        p.dma_ch("gpsimd", out[h][:, tile * TT:(tile + 1) * TT], ro,
                 s_out, waits=[(vv, omark)])



class L2:
    def __init__(self, S=8192, NHL=2):
        self.S = S
        self.NHL = NHL

    def build(self):
        S, NHL = self.S, self.NHL
        NQB = S // 512
        NC = S // 128
        nc = bass.Bass("TRN2", target_bir_lowering=False, debug=False,
                       detect_race_conditions=False)
        dr = {}
        dr["q2"] = nc.dram_tensor("q2", [NHL, 128, S], F32, kind="ExternalInput").ap()
        dr["k2"] = nc.dram_tensor("k2", [128, S], F32, kind="ExternalInput").ap()
        dr["v2"] = nc.dram_tensor("v2", [S, 128], F32, kind="ExternalInput").ap()
        dr["y2"] = nc.dram_tensor("y2", [NHL, 128, S], F32, kind="ExternalOutput").ap()

        with contextlib.ExitStack() as es:
            p = Prog(nc, es)
            q2 = p.sbuf("q2s", [128, NHL * S], F32)
            k2 = p.sbuf("k2s", [128, S], F32)
            v2 = p.sbuf("v2s", [128, S], F32)
            ee = p.sbuf("ee", [128, 2 * 512], F32)
            bc = p.sbuf("bc", [128, 512], F32)
            av = p.sbuf("av", [128, 512], F32)
            rr = p.sbuf("rr", [1, 512], F32)
            consts = p.sbuf("c2", [128, 132], F32)
            ps = [p.psum(f"ps{i}", [128, 512], F32) for i in range(8)]
            PS_AV, PS_R, PS_S0, PS_S1, PS_B = 0, 1, 2, 3, 4

            s_in = p.sem("s_in")
            s_out = p.sem("s_out")
            mm = p.sem("mms")
            vv = p.sem("vvs")
            aa = p.sem("aas")
            gg = p.sem("ggs")

            p.dma_ch("sync", q2[:, :].rearrange("p (h s) -> p h s", h=NHL), dr["q2"].rearrange("h p s -> p h s"), s_in)
            p.dma_ch("sync", k2[:, :], dr["k2"], s_in)
            p.dma_ch("sync", v2[:, :].rearrange("p (c n) -> p c n", c=NC),
                     dr["v2"].rearrange("(c p) n -> p c n", p=128), s_in)
            lds = s_in.count
            p.op("vector", lambda eng: eng.memset(consts[:, 0:1], 1.0), inc=(vv, 1))
            p.op("vector", lambda eng: eng.memset(consts[0:1, 1:129], 1.0), inc=(vv, 1))
            cready = vv.count
            ones_col = consts[:, 0:1]
            ones_row = consts[0:1, 1:129]

            ee_free = {}
            av_free = []
            sc_free = {}
            for qb in range(NQB):
                for h in range(NHL):
                    njc = 4 * qb + 4
                    qv = q2[:, h * S + qb * 512: h * S + (qb + 1) * 512]
                    for j in range(njc):
                        sb = PS_S0 if j % 2 == 0 else PS_S1
                        w0 = [(s_in, lds), (vv, cready)] if (qb == 0 and h == 0 and j < 2) else []
                        p.op("tensor", lambda eng, j=j, sb=sb, qv=qv: eng.matmul(
                            ps[sb][:, 0:512], k2[:, j * 128:(j + 1) * 128], qv,
                            start=True, stop=True),
                            waits=w0 + sc_free.get(sb, []), inc=(mm, 1))
                        smm = mm.count
                        par = j % 2
                        ev = ee[:, par * 512:(par + 1) * 512]
                        p.op("scalar", lambda eng, ev=ev, sb=sb: eng.activation(
                            ev, ps[sb][:, 0:512], AF.Exp),
                            waits=[(mm, smm)] + ee_free.get(par, []), inc=(aa, 1))
                        emark = aa.count
                        d = j - 4 * qb
                        if d >= 0:
                            p.op("gpsimd", lambda eng, ev=ev, d=d: eng.affine_select(
                                ev, ev, [[1, 512]], ALU.is_ge, 0.0,
                                base=-128 * d, channel_multiplier=-1),
                                waits=[(aa, emark)], inc=(gg, 1))
                            emark = None
                        ewait = [(gg, gg.count)] if emark is None else [(aa, emark)]
                        p.op("tensor", lambda eng, ev=ev, j=j, njc=njc: eng.matmul(
                            ps[PS_R][0:1, 0:512], ones_col, ev,
                            start=(j == 0), stop=(j == njc - 1)),
                            waits=ewait + (av_free if j == 0 else []), inc=(mm, 1))
                        p.op("tensor", lambda eng, ev=ev, j=j, njc=njc: eng.matmul(
                            ps[PS_AV][:, 0:512], v2[:, j * 128:(j + 1) * 128], ev,
                            start=(j == 0), stop=(j == njc - 1)), inc=(mm, 1))
                        sc_free[sb] = [(aa, aa.count)]
                        ee_free[par] = [(mm, mm.count)]
                    avm = mm.count
                    p.op("vector", lambda eng: eng.reciprocal(rr[:, :], ps[PS_R][0:1, 0:512]),
                         waits=[(mm, avm)], inc=(vv, 1))
                    rm = vv.count
                    p.op("tensor", lambda eng: eng.matmul(
                        ps[PS_B][:, 0:512], ones_row, rr[:, :], start=True, stop=True),
                        waits=[(vv, rm)], inc=(mm, 1))
                    p.op("vector", lambda eng: eng.tensor_copy(bc[:, :], ps[PS_B][:, 0:512]),
                         waits=[(mm, mm.count)], inc=(vv, 1))
                    p.op("vector", lambda eng: eng.tensor_mul(av[:, :], ps[PS_AV][:, 0:512], bc[:, :]),
                         inc=(vv, 1))
                    av_free = [(vv, vv.count)]
                    p.dma_ch("gpsimd", dr["y2"][h][:, qb * 512:(qb + 1) * 512], av[:, :],
                             s_out, waits=[(vv, vv.count)])
                    av_dma = s_out.count
                    av_free = av_free + [(s_out, av_dma)]
            p.op("gpsimd", lambda eng: eng.engine_nop(), waits=[(s_out, s_out.count)])
            p.build()
        return nc



class L3(L1):
    def build(self):
        T, TT, D, H = self.T, self.TT, self.D, self.H
        CD = self.CD
        nc = bass.Bass("TRN2", target_bir_lowering=False, debug=False,
                       detect_race_conditions=False)
        dr = {}
        dr["x_fm"] = nc.dram_tensor("x_fm", [D, T], F32, kind="ExternalInput").ap()
        dr["y_fm"] = nc.dram_tensor("y_fm", [D, T], F32, kind="ExternalInput").ap()
        dr["wgT"] = nc.dram_tensor("wgT", [D, H], F32, kind="ExternalInput").ap()
        dr["wuT"] = nc.dram_tensor("wuT", [D, H], F32, kind="ExternalInput").ap()
        dr["wdT"] = nc.dram_tensor("wdT", [H, D], F32, kind="ExternalInput").ap()
        dr["wpT"] = nc.dram_tensor("wpT", [D, D], F32, kind="ExternalInput").ap()
        dr["sel16_in"] = nc.dram_tensor("sel16_in", [16, 16 * 128], F32, kind="ExternalInput").ap()
        dr["selc_in"] = nc.dram_tensor("selc_in", [128, 2 * CD - 1], BF16, kind="ExternalInput").ap()
        dr["x3_out"] = nc.dram_tensor("x3_out", [D, T], F32, kind="ExternalOutput").ap()
        with contextlib.ExitStack() as es:
            p = Prog(nc, es)
            self._body3(nc, p, dr)
            p.build()
        return nc

    def _body3(self, nc, p, dr):
        T, TT, D, H = self.T, self.TT, self.D, self.H
        CD, CH, NT = self.CD, self.CH, self.NT
        SLAB = 256
        NSLAB = D // SLAB

        xT = p.sbuf("xT", [128, CD * T], F32)
        h1 = p.sbuf("h1", [128, CD * TT], F32)
        act = p.sbuf("act", [128, 2 * TT], F32)
        y_acc = p.sbuf("y_acc", [128, CD * TT], F32)
        wg_s = p.sbuf("wg_s", [128, 2 * D], F32)
        wu_s = p.sbuf("wu_s", [128, 2 * D], F32)
        wd_s = p.sbuf("wd_s", [128, 2 * D], F32)
        sq = p.sbuf("sq", [128, 2 * TT], F32)
        stats = p.sbuf("stats", [128, 3 * TT], F32)
        consts = p.sbuf("consts", [128, 144], F32)
        selcol = p.sbuf("selcol", [128, 2 * CD - 1], BF16)
        scl_s = p.sbuf("scl_s", [16, 896], F32)
        scl16 = scl_s[:, 0:256]
        scf16 = scl_s[:, 256:512]
        inv16 = scl_s[:, 512:768]
        sclb = scl_s[:, 768:896].bitcast(BF16)
        sel16 = p.sbuf("sel16", [16, 16 * 128], F32)
        weff_bufs = (wd_s, y_acc[:, 0:4096])
        wbb = y_acc[:, 4096:6144].bitcast(BF16)
        wab = y_acc[:, 6144:8192].bitcast(BF16)
        qsc = p.sbuf("qsc", [128, 640], F32)
        rb = qsc[:, 0:128].bitcast(BF16)
        g1 = qsc[:, 128:384]
        g2 = qsc[:, 384:640]

        ps = [p.psum(f"ps{i}", [128, 512], F32) for i in range(8)]
        PS_GATE, PS_UP, PS_D0, PS_D1, PS_S0, PS_S1, PS_B0, PS_B1 = range(8)

        s_in = p.sem("s_in")
        s_gu = [p.sem("s_gu0"), p.sem("s_gu1")]
        s_wd = [p.sem("s_wd0"), p.sem("s_wd1")]
        s_st = [p.sem("s_st0"), p.sem("s_st1")]
        s_y = p.sem("s_y")
        s_x1 = p.sem("s_x1")
        mm = p.sem("mms")
        vv = p.sem("vvs")
        aa = p.sem("aas")

        self._sqpar = 0
        self._sq_marks = {}
        self._free = {}
        self._weff_dump = {}

        p.dma_ch("sync", xT[:, :].rearrange("p (c t) -> p c t", c=CD),
                 dr["x_fm"].rearrange("(c p) t -> p c t", p=128), s_in)
        p.dma_ch("sync", selcol[:, :], dr["selc_in"], s_in)
        p.dma_ch("sync", sel16[:, :], dr["sel16_in"], s_in)
        lds0 = s_in.count
        p.op("vector", lambda eng: eng.memset(consts[:, 0:1], 1.0), inc=(vv, 1))
        p.op("vector", lambda eng: eng.memset(consts[0:1, 1:129], 1.0), inc=(vv, 1))
        consts_ready = vv.count
        ones_col = consts[:, 0:1]
        ones_row = consts[0:1, 1:129]
        A = stats[0:1, 0:TT]
        B = stats[0:1, TT:2 * TT]
        C = stats[0:1, 2 * TT:3 * TT]

        def rstd_from_B(extra_eps, width):
            p.op("vector", lambda eng: eng.tensor_scalar_add(
                B[:, 0:width], B[:, 0:width], extra_eps), inc=(vv, 1))
            p.op("vector", lambda eng: eng.reciprocal(
                C[:, 0:width], B[:, 0:width]), inc=(vv, 1))
            p.op("scalar", lambda eng: eng.activation(
                C[:, 0:width], C[:, 0:width], AF.Sqrt),
                waits=[(vv, vv.count)], inc=(aa, 1))
            p.op("scalar", lambda eng: eng.activation(
                A[:, 0:width], C[:, 0:width], AF.Square), inc=(aa, 1))
            p.op("vector", lambda eng: eng.tensor_mul(
                A[:, 0:width], A[:, 0:width], B[:, 0:width]),
                waits=[(aa, aa.count)], inc=(vv, 1))
            p.op("vector", lambda eng: eng.tensor_scalar(
                A[:, 0:width], A[:, 0:width], -0.5, 1.5, ALU.mult, ALU.add),
                inc=(vv, 1))
            p.op("vector", lambda eng: eng.tensor_mul(
                C[:, 0:width], C[:, 0:width], A[:, 0:width]), inc=(vv, 1))
            return vv.count

        def ln(src_view, dst_view, eps, wait0=(), n_chunks=CD, width=TT):
            w0 = list(wait0) + [(vv, consts_ready)]
            ndim = n_chunks * 128
            for c in range(n_chunks):
                par = self._sqpar
                self._sqpar ^= 1
                wsq = [(mm, self._sq_marks[par])] if par in self._sq_marks else []
                p.op("scalar", lambda eng, c=c, par=par: eng.activation(
                    sq[:, par * TT:par * TT + width], src_view(c), AF.Square),
                    waits=wsq + (w0 if c == 0 else []), inc=(aa, 1))
                p.op("tensor", lambda eng, c=c: eng.matmul(
                    ps[PS_S0][0:1, 0:width], ones_col, src_view(c),
                    start=(c == 0), stop=(c == n_chunks - 1)),
                    waits=(w0 + self._free.get(PS_S0, [])) if c == 0 else (),
                    inc=(mm, 1))
                p.op("tensor", lambda eng, c=c, par=par: eng.matmul(
                    ps[PS_S1][0:1, 0:width], ones_col,
                    sq[:, par * TT:par * TT + width],
                    start=(c == 0), stop=(c == n_chunks - 1)),
                    waits=[(aa, aa.count)] + (self._free.get(PS_S1, []) if c == 0 else []),
                    inc=(mm, 1))
                self._sq_marks[par] = mm.count
            stats_mm = mm.count
            w = [(mm, stats_mm)]
            p.op("vector", lambda eng: eng.tensor_scalar_mul(
                A[:, 0:width], ps[PS_S0][0:1, 0:width], 1.0 / ndim),
                waits=w, inc=(vv, 1))
            self._free[PS_S0] = [(vv, vv.count)]
            p.op("vector", lambda eng: eng.tensor_scalar_mul(
                B[:, 0:width], ps[PS_S1][0:1, 0:width], 1.0 / ndim),
                waits=w, inc=(vv, 1))
            self._free[PS_S1] = [(vv, vv.count)]
            mean_done = vv.count
            p.op("tensor", lambda eng: eng.matmul(
                ps[PS_B0][:, 0:width], ones_row, A[:, 0:width],
                start=True, stop=True),
                waits=[(vv, mean_done)] + self._free.get(PS_B0, []), inc=(mm, 1))
            mbc = mm.count
            p.op("scalar", lambda eng: eng.activation(
                C[:, 0:width], A[:, 0:width], AF.Square),
                waits=[(vv, mean_done)], inc=(aa, 1))
            p.op("vector", lambda eng: eng.tensor_sub(
                B[:, 0:width], B[:, 0:width], C[:, 0:width]),
                waits=[(aa, aa.count), (mm, mbc)], inc=(vv, 1))
            rstd_done = rstd_from_B(eps, width)
            p.op("tensor", lambda eng: eng.matmul(
                ps[PS_B1][:, 0:width], ones_row, C[:, 0:width],
                start=True, stop=True),
                waits=[(vv, rstd_done)] + self._free.get(PS_B1, []), inc=(mm, 1))
            bca = mm.count
            for c in range(n_chunks):
                p.op("vector", lambda eng, c=c: eng.tensor_sub(
                    dst_view(c), src_view(c), ps[PS_B0][:, 0:width]),
                    waits=[(mm, bca)] if c == 0 else (), inc=(vv, 1))
                p.op("vector", lambda eng, c=c: eng.tensor_mul(
                    dst_view(c), dst_view(c), ps[PS_B1][:, 0:width]), inc=(vv, 1))
            self._free[PS_B0] = [(vv, vv.count)]
            self._free[PS_B1] = [(vv, vv.count)]
            return vv.count, bca

        def xv(tile):
            return lambda c: xT[:, c * T + tile * TT: c * T + (tile + 1) * TT]

        def h1v(c):
            return h1[:, c * TT:(c + 1) * TT]

        gu_marks = {}
        wd_marks = {}
        gate_free = aa.count
        up_free = vv.count
        dn_free = {}
        x1dma = []
        l3_stage_guard = {}
        l3_weff_guard = {}

        def load_gu(tile, h):
            gseq = tile * CH + h
            w = list([(mm, gu_marks[gseq - 2])] if gseq - 2 in gu_marks else [])
            if h < 2 and tile in l3_stage_guard:
                w += l3_stage_guard[tile]
            buf = gseq % 2
            sem = s_gu[buf]
            p.dma_ch("sync", wg_s[:, buf * D:(buf + 1) * D].rearrange("p (c n) -> p c n", c=CD),
                     dr["wgT"][:, h * 128:(h + 1) * 128].rearrange("(c p) n -> p c n", p=128),
                     sem, waits=w)
            p.dma_ch("sync", wu_s[:, buf * D:(buf + 1) * D].rearrange("p (c n) -> p c n", c=CD),
                     dr["wuT"][:, h * 128:(h + 1) * 128].rearrange("(c p) n -> p c n", p=128),
                     sem)
            return (sem, sem.count)

        def load_wd(tile, g):
            gseq = tile * (CH // 2) + g
            w = list([(mm, wd_marks[gseq - 1])] if gseq - 1 in wd_marks else [])
            if g == 0 and tile in l3_weff_guard:
                w += l3_weff_guard[tile]
            sem = s_wd[gseq % 2]
            for i in range(2):
                h = 2 * g + i
                p.dma_ch("sync",
                         wd_s[:, i * D:(i + 1) * D].rearrange("p (c n) -> p c n", c=CD),
                         dr["wdT"][h * 128:(h + 1) * 128, :].rearrange("p (c n) -> p c n", c=CD),
                         sem, waits=w if i == 0 else ())
            return (sem, sem.count)

        stage_bufs = (wg_s, wu_s)
        for tile in range(NT):
            p.dma_ch("sync", h1[:, :].rearrange("p (c t) -> p c t", c=CD),
                     dr["y_fm"].rearrange("(c p) t -> p c t", p=128)[:, :, tile * TT:(tile + 1) * TT],
                     s_y, waits=[(vv, up_free)] if tile else [])
            yld = s_y.count
            for c in range(CD):
                par = self._sqpar
                self._sqpar ^= 1
                wsq = [(mm, self._sq_marks[par])] if par in self._sq_marks else []
                p.op("scalar", lambda eng, c=c, par=par: eng.activation(
                    sq[:, par * TT:(par + 1) * TT], h1v(c), AF.Square),
                    waits=wsq + ([(s_y, yld), (vv, consts_ready)] if c == 0 else []),
                    inc=(aa, 1))
                p.op("tensor", lambda eng, c=c, par=par: eng.matmul(
                    ps[PS_S1][0:1, 0:TT], ones_col, sq[:, par * TT:(par + 1) * TT],
                    start=(c == 0), stop=(c == CD - 1)),
                    waits=[(aa, aa.count)] + (self._free.get(PS_S1, []) if c == 0 else []),
                    inc=(mm, 1))
                self._sq_marks[par] = mm.count
            smm = mm.count
            p.op("vector", lambda eng: eng.tensor_scalar_mul(
                B, ps[PS_S1][0:1, 0:TT], 1.0 / D), waits=[(mm, smm)], inc=(vv, 1))
            self._free[PS_S1] = [(vv, vv.count)]
            rstd_done = rstd_from_B(EPS_RMS, TT)
            p.op("tensor", lambda eng: eng.matmul(
                ps[PS_B1][:, 0:TT], ones_row, C, start=True, stop=True),
                waits=[(vv, rstd_done)] + self._free.get(PS_B1, []), inc=(mm, 1))
            bca = mm.count
            for c in range(CD):
                p.op("vector", lambda eng, c=c: eng.tensor_mul(
                    h1v(c), h1v(c), ps[PS_B1][:, 0:TT]),
                    waits=[(mm, bca)] if c == 0 else (), inc=(vv, 1))
            self._free[PS_B1] = [(vv, vv.count)]
            yn_ready = vv.count

            stage_ld = {}
            stage_read = {}
            weff_done = {}
            wb_read = {}

            def load_slab(s, tile=tile, stage_ld=stage_ld, stage_read=stage_read):
                w = []
                if s - 2 in stage_read:
                    w.append((aa, stage_read[s - 2]))
                elif tile == 0 and s < 2:
                    w.append((vv, consts_ready))
                sem = s_st[s % 2]
                p.dma_ch("sync",
                         stage_bufs[s % 2][:, 0:CD * SLAB].rearrange("p (c n) -> p c n", c=CD),
                         dr["wpT"][:, s * SLAB:(s + 1) * SLAB].rearrange("(c p) n -> p c n", p=128),
                         sem, waits=w)
                stage_ld[s] = (sem, sem.count)

            load_slab(0)
            load_slab(1)
            for s in range(NSLAB):
                par = s % 2
                stage = stage_bufs[par]
                weff = weff_bufs[par]
                w = [stage_ld[s]]
                if s - 1 in wb_read:
                    w.append((vv, wb_read[s - 1]))
                if s == 0 and tile == 0:
                    w.append((s_in, lds0))
                if s == 0 and tile == 1:
                    w.append((vv, yn_ready))
                p.op("scalar", lambda eng, stage=stage: eng.copy(
                    wbb[:, :], stage[:, 0:CD * SLAB]), waits=w, inc=(aa, 1))
                p.op("scalar", lambda eng, stage=stage: eng.activation(
                    wab[:, :], stage[:, 0:CD * SLAB], AF.Abs), inc=(aa, 1))
                stage_read[s] = aa.count
                if s + 2 < NSLAB:
                    load_slab(s + 2)
                for c in range(CD):
                    p.op("tensor", lambda eng, c=c: eng.matmul(
                        ps[PS_S0][0:16, 0:SLAB], selcol[:, (CD - 1 - c):(2 * CD - 1 - c)],
                        wab[:, c * SLAB:(c + 1) * SLAB],
                        start=(c == 0), stop=(c == CD - 1)),
                        waits=([(aa, stage_read[s])] + self._free.get(PS_S0, []))
                        if c == 0 else (), inc=(mm, 1))
                gsum_mm = mm.count
                p.op("vector", lambda eng: eng.tensor_scalar_mul(
                    scl16, ps[PS_S0][0:16, 0:SLAB], 1.0 / 128.0),
                    waits=[(mm, gsum_mm)], inc=(vv, 1))
                self._free[PS_S0] = [(vv, vv.count)]
                p.op("vector", lambda eng: eng.tensor_copy(sclb, scl16), inc=(vv, 1))
                p.op("vector", lambda eng: eng.tensor_copy(scf16, sclb), inc=(vv, 1))
                p.op("vector", lambda eng: eng.tensor_scalar_max(
                    scf16, scf16, 1e-8), inc=(vv, 1))
                p.op("vector", lambda eng: eng.reciprocal(inv16, scf16), inc=(vv, 1))
                scl_done = vv.count
                for c in range(CD):
                    p.op("tensor", lambda eng, c=c: eng.matmul(
                        ps[PS_B0][:, 0:SLAB], sel16[:, c * 128:(c + 1) * 128],
                        inv16, start=True, stop=True),
                        waits=([(vv, scl_done)] if c == 0 else [])
                        + self._free.get(PS_B0, []), inc=(mm, 1))
                    p.op("tensor", lambda eng, c=c: eng.matmul(
                        ps[PS_B1][:, 0:SLAB], sel16[:, c * 128:(c + 1) * 128],
                        scf16, start=True, stop=True),
                        waits=self._free.get(PS_B1, []), inc=(mm, 1))
                    bc = mm.count
                    p.op("vector", lambda eng, c=c: eng.tensor_mul(
                        rb, wbb[:, c * SLAB:(c + 1) * SLAB], ps[PS_B0][:, 0:SLAB]),
                        waits=[(mm, bc)], inc=(vv, 1))
                    self._free[PS_B0] = [(vv, vv.count)]
                    p.op("vector", lambda eng: eng.tensor_single_scalar(
                        g1, rb, 0.5, ALU.is_gt), inc=(vv, 1))
                    p.op("vector", lambda eng: eng.tensor_single_scalar(
                        g2, rb, -0.5, ALU.is_lt), inc=(vv, 1))
                    p.op("vector", lambda eng: eng.tensor_sub(g1, g1, g2), inc=(vv, 1))
                    wwf = [(mm, weff_done[s - 2])] if (c == 0 and s - 2 in weff_done) else []
                    p.op("vector", lambda eng, c=c, weff=weff: eng.tensor_mul(
                        weff[:, c * SLAB:(c + 1) * SLAB], g1, ps[PS_B1][:, 0:SLAB]),
                        waits=wwf, inc=(vv, 1))
                    self._free[PS_B1] = [(vv, vv.count)]
                wb_read[s] = vv.count
                weff_ready = vv.count
                for oc in range(2):
                    bank = PS_GATE if oc == 0 else PS_UP
                    og = s * 2 + oc
                    for k in range(CD):
                        p.op("tensor",
                             lambda eng, k=k, oc=oc, weff=weff, bank=bank:
                             eng.matmul(
                                 ps[bank][:, 0:TT],
                                 weff[:, k * SLAB + oc * 128: k * SLAB + oc * 128 + 128],
                                 h1v(k), start=(k == 0), stop=(k == CD - 1)),
                             waits=[(vv, weff_ready), (vv, vv.count), (vv, yn_ready),
                                    (aa, gate_free)]
                             if k == 0 else (), inc=(mm, 1) if k == CD - 1 else None)
                    pmm = mm.count
                    p.op("vector", lambda eng, og=og, tile=tile, bank=bank:
                         eng.tensor_add(xv(tile)(og), xv(tile)(og), ps[bank][:, 0:TT]),
                         waits=[(mm, pmm)], inc=(vv, 1))
                weff_done[s] = mm.count
            l3_stage_guard[tile] = [(aa, stage_read[NSLAB - 1]),
                                    (aa, stage_read[NSLAB - 2])]
            l3_weff_guard[tile] = [(mm, weff_done[NSLAB - 1]),
                                   (mm, weff_done[NSLAB - 2])]

            h1_ready, _ = ln(xv(tile), h1v, EPS_LN)
            gu_ld = {0: load_gu(tile, 0), 1: load_gu(tile, 1)}
            wd_ld = {0: load_wd(tile, 0)}
            for h in range(CH):
                gseq = tile * CH + h
                buf = gseq % 2
                gi = h % 2
                g = h // 2
                for k in range(CD):
                    p.op("tensor", lambda eng, k=k, buf=buf: eng.matmul(
                        ps[PS_GATE][:, 0:TT],
                        wg_s[:, buf * D + k * 128: buf * D + (k + 1) * 128],
                        h1v(k), start=(k == 0), stop=(k == CD - 1)),
                        waits=[gu_ld[h], (vv, h1_ready), (aa, gate_free)]
                        if k == 0 else (), inc=(mm, 1) if k == CD - 1 else None)
                gate_mm = mm.count
                for k in range(CD):
                    p.op("tensor", lambda eng, k=k, buf=buf: eng.matmul(
                        ps[PS_UP][:, 0:TT],
                        wu_s[:, buf * D + k * 128: buf * D + (k + 1) * 128],
                        h1v(k), start=(k == 0), stop=(k == CD - 1)),
                        waits=[(vv, up_free)] if k == 0 else (),
                        inc=(mm, 1) if k == CD - 1 else None)
                gu_marks[gseq] = mm.count
                up_mm = mm.count
                if h + 2 < CH:
                    gu_ld[h + 2] = load_gu(tile, h + 2)
                p.op("scalar", lambda eng, gi=gi: eng.activation(
                    act[:, gi * TT:(gi + 1) * TT], ps[PS_GATE][:, 0:TT], AF.Silu),
                    waits=[(mm, gate_mm)], inc=(aa, 1))
                gate_free = aa.count
                p.op("vector", lambda eng, gi=gi: eng.tensor_mul(
                    act[:, gi * TT:(gi + 1) * TT], act[:, gi * TT:(gi + 1) * TT],
                    ps[PS_UP][:, 0:TT]),
                    waits=[(mm, up_mm), (aa, aa.count)], inc=(vv, 1))
                up_free = vv.count
                act_ready = vv.count
                if gi == 1:
                    gsq = tile * (CH // 2) + g
                    for o in range(CD):
                        bank = PS_D0 if o % 2 == 0 else PS_D1
                        wdn = list([(vv, dn_free[o - 2])] if o - 2 in dn_free else [])
                        if o == 0:
                            wdn += [(vv, act_ready), wd_ld[g]]
                        for i in range(2):
                            p.op("tensor", lambda eng, o=o, i=i, bank=bank: eng.matmul(
                                ps[bank][:, 0:TT],
                                wd_s[:, i * D + o * 128: i * D + (o + 1) * 128],
                                act[:, i * TT:(i + 1) * TT],
                                start=(i == 0), stop=(i == 1)),
                                waits=wdn if i == 0 else (),
                                inc=(mm, 1) if i == 1 else None)
                        dn_mm = mm.count
                        if g == 0:
                            p.op("vector", lambda eng, o=o, bank=bank: eng.tensor_copy(
                                y_acc[:, o * TT:(o + 1) * TT], ps[bank][:, 0:TT]),
                                waits=[(mm, dn_mm)], inc=(vv, 1))
                        else:
                            p.op("vector", lambda eng, o=o, bank=bank: eng.tensor_add(
                                y_acc[:, o * TT:(o + 1) * TT],
                                y_acc[:, o * TT:(o + 1) * TT], ps[bank][:, 0:TT]),
                                waits=[(mm, dn_mm)], inc=(vv, 1))
                        dn_free[o] = vv.count
                        self._free[bank] = [(vv, vv.count)]
                    wd_marks[gsq] = mm.count
                    if g + 1 < CH // 2:
                        wd_ld[g + 1] = load_wd(tile, g + 1)
            for o in range(CD):
                p.op("vector", lambda eng, o=o, tile=tile: eng.scalar_tensor_tensor(
                    xv(tile)(o), y_acc[:, o * TT:(o + 1) * TT], 0.5,
                    xv(tile)(o), ALU.mult, ALU.add), inc=(vv, 1))
            x3_done = vv.count
            p.dma_ch("gpsimd",
                     dr["x3_out"].rearrange("(c p) t -> p c t", p=128)[:, :, tile * TT:(tile + 1) * TT],
                     xT[:, :].rearrange("p (c t) -> p c t", c=CD)[:, :, tile * TT:(tile + 1) * TT],
                     s_x1, waits=[(vv, x3_done)])
            x1dma.append(s_x1.count)
        p.op("gpsimd", lambda eng: eng.engine_nop(), waits=[(s_x1, s_x1.count)])



NCORE = 8
SEQ = 8192
DM = 2048
TCORE = SEQ // NCORE


def _consts():
    eye = np.eye(128, dtype=np.float32)
    sel16 = np.zeros((16, 16 * 128), dtype=np.float32)
    for c in range(16):
        sel16[c, c * 128:(c + 1) * 128] = 1.0
    CD = DM // 128
    import ml_dtypes
    selc = np.zeros((128, 2 * CD - 1), dtype=ml_dtypes.bfloat16)
    for pp in range(128):
        selc[pp, CD - 1] = 1.0
    R = np.zeros((128, 128), dtype=np.float32)
    for i in range(64):
        R[i, 64 + i] = 1.0
        R[64 + i, i] = -1.0
    rotc = np.ascontiguousarray(R.T)
    return eye, sel16, selc, rotc


def _rope_tables(pos):
    inv = 1.0 / (10000.0 ** (np.arange(0, 128, 2) / 128.0))
    fr = np.outer(pos.astype(np.float64), inv)
    c = np.cos(fr).T.astype(np.float32)
    s = np.sin(fr).T.astype(np.float32)
    return np.concatenate([c, c]), np.concatenate([s, s])


def kernel(L0_attn_x, L0_attn_w_qkv, L0_attn_w_proj, L0_attn_q_gain,
           L0_ffn_w_gate, L0_ffn_w_up, L0_ffn_w_down):
    from concourse.bass_utils import run_bass_kernel_spmd

    x = np.ascontiguousarray(np.asarray(L0_attn_x, dtype=np.float32)[0].T)
    wgT = np.ascontiguousarray(np.asarray(L0_ffn_w_gate, np.float32).T)
    wuT = np.ascontiguousarray(np.asarray(L0_ffn_w_up, np.float32).T)
    wdT = np.ascontiguousarray(np.asarray(L0_ffn_w_down, np.float32).T)
    wqkvT = np.ascontiguousarray(np.asarray(L0_attn_w_qkv, np.float32).T)
    wpT = np.ascontiguousarray(np.asarray(L0_attn_w_proj, np.float32).T)
    gain = np.asarray(L0_attn_q_gain, np.float32)[None, :]
    eye, sel16, selc, rotc = _consts()

    im1 = []
    for c in range(NCORE):
        pos = np.arange(c * TCORE, (c + 1) * TCORE)
        cos, sin = _rope_tables(pos)
        im1.append({
            "x_fm": np.ascontiguousarray(x[:, c * TCORE:(c + 1) * TCORE]),
            "wgT": wgT, "wuT": wuT, "wdT": wdT, "wqkvT": wqkvT,
            "cos_t": cos, "sin_t": sin, "gain_in": gain,
            "eye_in": eye, "sel16_in": sel16, "selc_in": selc, "rotc_in": rotc,
        })
    nc1 = L1().build()
    r1 = run_bass_kernel_spmd(nc1, im1, core_ids=list(range(NCORE))).results

    q_all = np.concatenate([r["q_out"] for r in r1], axis=2)
    k_all = np.concatenate([r["k_out"] for r in r1], axis=2)
    v_all = np.concatenate([r["v_out"] for r in r1], axis=1)

    im2 = []
    for c in range(NCORE):
        kv = c // 2
        im2.append({
            "q2": np.ascontiguousarray(q_all[2 * c:2 * c + 2]),
            "k2": np.ascontiguousarray(k_all[kv]),
            "v2": np.ascontiguousarray(v_all[kv]),
        })
    nc2 = L2().build()
    r2 = run_bass_kernel_spmd(nc2, im2, core_ids=list(range(NCORE))).results
    y_all = np.concatenate([r["y2"] for r in r2], axis=0)
    y_fm = y_all.reshape(DM, SEQ)

    im3 = []
    for c in range(NCORE):
        im3.append({
            "x_fm": r1[c]["x1_out"],
            "y_fm": np.ascontiguousarray(y_fm[:, c * TCORE:(c + 1) * TCORE]),
            "wgT": wgT, "wuT": wuT, "wdT": wdT, "wpT": wpT,
            "sel16_in": sel16, "selc_in": selc,
        })
    nc3 = L3().build()
    r3 = run_bass_kernel_spmd(nc3, im3, core_ids=list(range(NCORE))).results
    out = np.concatenate([r["x3_out"].T for r in r3], axis=0)
    return np.ascontiguousarray(out[None, :, :])


# version 02
# speedup vs baseline: 1.0739x; 1.0739x over previous
import contextlib
import numpy as np
import concourse.bass as bass
import concourse.mybir as mybir

F32 = mybir.dt.float32
BF16 = mybir.dt.bfloat16
I32 = mybir.dt.int32
AF = mybir.ActivationFunctionType
ALU = mybir.AluOpType
AX = mybir.AxisListType

ENGINES = ("sync", "tensor", "vector", "scalar", "gpsimd")


class Sem:
    __slots__ = ("name", "count", "h")

    def __init__(self, name, h):
        self.name = name
        self.count = 0
        self.h = h


class Prog:

    def __init__(self, nc, es):
        self.nc = nc
        self.es = es
        self.ops = []

    def sem(self, name):
        return Sem(name, self.es.enter_context(self.nc.semaphore(name)))

    def sbuf(self, name, shape, dtype):
        return self.es.enter_context(self.nc.sbuf_tensor(name, shape, dtype))

    def psum(self, name, shape, dtype=F32):
        return self.es.enter_context(self.nc.psum_tensor(name, shape, dtype))

    def op(self, engine, fn, waits=(), inc=None, incs=None):
        assert engine in ENGINES, engine
        for s, v in waits:
            assert v <= s.count, (s.name, v, s.count, "wait on future sem value")
        ilist = list(incs) if incs else ([] if inc is None else [inc])
        for s, n in ilist:
            s.count += n
        self.ops.append((engine, tuple(waits), fn, ilist))

    def dma(self, engine, out, in_, waits=(), inc=None):
        def fn(eng, out=out, in_=in_):
            return eng.dma_start(out=out, in_=in_)
        self.op(engine, fn, waits=waits, inc=inc)

    def dma_ch(self, engine, out, in_, sem, waits=()):
        def fn(eng, out=out, in_=in_):
            return eng.dma_start(out=out, in_=in_)
        self.op(engine, fn, waits=waits, inc=(sem, 16))

    def build(self):
        with self.nc.Block() as block:
            for name in ENGINES:
                ops = [o for o in self.ops if o[0] == name]
                if not ops:
                    continue

                def make(ops):
                    def emit(eng):
                        for _, waits, fn, ilist in ops:
                            for s, v in waits:
                                eng.wait_ge(s.h, v)
                            ins = fn(eng)
                            for s, n in ilist:
                                ins = ins.then_inc(s.h, n)
                    return emit

                getattr(block, name)(make(ops))


def ceil_div(a, b):
    return -(-a // b)



RSQRT_HD = 0.08838834764831845
EPS_LN = 1e-5
EPS_RMS = 1.1920929e-07


class L1:
    def __init__(self, T=1024, TT=512, D=2048, H=8192, NH=16, NKV=4):
        self.__dict__.update(T=T, TT=TT, D=D, H=H, NH=NH, NKV=NKV)
        self.CD, self.CH, self.NT = D // 128, H // 128, T // TT
        self.O3 = (NH + 2 * NKV) * 128

    def build(self):
        T, TT, D, H, NH, NKV = self.T, self.TT, self.D, self.H, self.NH, self.NKV
        CD, O3 = self.CD, self.O3
        nc = bass.Bass("TRN2", target_bir_lowering=False, debug=False,
                       detect_race_conditions=False)

        dr = {}
        dr["x_fm"] = nc.dram_tensor("x_fm", [D, T], F32, kind="ExternalInput").ap()
        dr["wgT"] = nc.dram_tensor("wgT", [D, H], F32, kind="ExternalInput").ap()
        dr["wuT"] = nc.dram_tensor("wuT", [D, H], F32, kind="ExternalInput").ap()
        dr["wdT"] = nc.dram_tensor("wdT", [H, D], F32, kind="ExternalInput").ap()
        dr["wqkvT"] = nc.dram_tensor("wqkvT", [D, O3], F32, kind="ExternalInput").ap()
        dr["cos_t"] = nc.dram_tensor("cos_t", [128, T], F32, kind="ExternalInput").ap()
        dr["sin_t"] = nc.dram_tensor("sin_t", [128, T], F32, kind="ExternalInput").ap()
        dr["rotc_in"] = nc.dram_tensor("rotc_in", [128, 128], F32, kind="ExternalInput").ap()
        dr["gain_in"] = nc.dram_tensor("gain_in", [1, NH], F32, kind="ExternalInput").ap()
        dr["eye_in"] = nc.dram_tensor("eye_in", [128, 128], F32, kind="ExternalInput").ap()
        dr["sel16_in"] = nc.dram_tensor("sel16_in", [16, 16 * 128], F32, kind="ExternalInput").ap()
        dr["selc_in"] = nc.dram_tensor("selc_in", [128, 2 * CD - 1], BF16, kind="ExternalInput").ap()

        dr["x1_out"] = nc.dram_tensor("x1_out", [D, T], F32, kind="ExternalOutput").ap()
        dr["q_out"] = nc.dram_tensor("q_out", [NH, 128, T], F32, kind="ExternalOutput").ap()
        dr["k_out"] = nc.dram_tensor("k_out", [NKV, 128, T], F32, kind="ExternalOutput").ap()
        dr["v_out"] = nc.dram_tensor("v_out", [NKV, T, 128], F32, kind="ExternalOutput").ap()
        if getattr(self, "debug_weff", False):
            dr["dbg_weff"] = nc.dram_tensor("dbg_weff", [self.O3 // 256, 128, 16 * 256], F32,
                                            kind="ExternalOutput").ap()
            dr["dbg_wb"] = nc.dram_tensor("dbg_wb", [self.O3 // 256, 128, 16 * 256], F32,
                                          kind="ExternalOutput").ap()

        with contextlib.ExitStack() as es:
            p = Prog(nc, es)
            self._body(nc, p, dr)
            p.build()
        return nc

    def _body(self, nc, p, dr):
        T, TT, D, H, NH, NKV = self.T, self.TT, self.D, self.H, self.NH, self.NKV
        CD, CH, NT, O3 = self.CD, self.CH, self.NT, self.O3
        SLAB = 256
        NSLAB = O3 // SLAB

        xT = p.sbuf("xT", [128, CD * T], F32)
        h1 = p.sbuf("h1", [128, CD * TT], F32)
        act = p.sbuf("act", [128, 2 * TT], F32)
        y_acc = p.sbuf("y_acc", [128, CD * TT], F32)
        wg_s = p.sbuf("wg_s", [128, 2 * D], F32)
        wu_s = p.sbuf("wu_s", [128, 2 * D], F32)
        wd_s = p.sbuf("wd_s", [128, 2 * D], F32)
        cos2 = p.sbuf("cos2", [128, T], F32)
        sin2 = p.sbuf("sin2", [128, T], F32)
        rotc = p.sbuf("rotc", [128, 128], F32)
        sq = p.sbuf("sq", [128, 2 * TT], F32)
        stats = p.sbuf("stats", [128, 3 * TT], F32)
        consts = p.sbuf("consts", [128, 144], F32)
        gain_s = p.sbuf("gain_s", [1, NH], F32)
        selcol = p.sbuf("selcol", [128, 2 * CD - 1], BF16)
        eye_s = p.sbuf("eye_s", [128, 128], F32)

        wbb = y_acc[:, 0:CD * SLAB // 2].bitcast(BF16)
        wab = y_acc[:, 2048:4096].bitcast(BF16)
        rb = y_acc[:, 4096:4224].bitcast(BF16)
        g1 = y_acc[:, 4352:4608]
        g2 = y_acc[:, 4608:4864]
        qraw = y_acc[:, 4864:5376]
        ro = y_acc[:, 5376:5888]
        scl_s = p.sbuf("scl_s", [16, 896], F32)
        scl16 = scl_s[:, 0:256]
        scf16 = scl_s[:, 256:512]
        inv16 = scl_s[:, 512:768]
        sclb = scl_s[:, 768:896].bitcast(BF16)
        sel16 = y_acc[0:16, 5888:7936]

        ps = [p.psum(f"ps{i}", [128, 512], F32) for i in range(8)]
        PS_GATE, PS_UP, PS_D0, PS_D1, PS_S0, PS_S1, PS_B0, PS_B1 = range(8)

        s_in = p.sem("s_in")
        s_gu = [p.sem("s_gu0"), p.sem("s_gu1")]
        s_wd = [p.sem("s_wd0"), p.sem("s_wd1")]
        s_st = [p.sem("s_st0"), p.sem("s_st1")]
        s_x1 = p.sem("s_x1")
        s_out = p.sem("s_out")
        s_dbg = p.sem("s_dbg")
        mm = p.sem("mms")
        vv = p.sem("vvs")
        aa = p.sem("aas")

        self._sqpar = 0
        self._sq_marks = {}
        self._free = {}
        self._weff_dump = {}

        p.dma_ch("sync", xT[:, :].rearrange("p (c t) -> p c t", c=CD),
                 dr["x_fm"].rearrange("(c p) t -> p c t", p=128), s_in)
        p.dma_ch("sync", cos2[:, :], dr["cos_t"], s_in)
        p.dma_ch("sync", sin2[:, :], dr["sin_t"], s_in)
        p.dma_ch("sync", rotc[:, :], dr["rotc_in"], s_in)
        p.dma_ch("sync", gain_s[:, :], dr["gain_in"], s_in)
        p.dma_ch("sync", eye_s[:, :], dr["eye_in"], s_in)
        p.dma_ch("sync", selcol[:, :], dr["selc_in"], s_in)
        lds0 = s_in.count
        p.op("vector", lambda eng: eng.memset(consts[:, 0:1], 1.0), inc=(vv, 1))
        p.op("vector", lambda eng: eng.memset(consts[0:1, 1:129], 1.0), inc=(vv, 1))
        consts_ready = vv.count
        ones_col = consts[:, 0:1]
        ones_row = consts[0:1, 1:129]

        A = stats[0:1, 0:TT]
        B = stats[0:1, TT:2 * TT]
        C = stats[0:1, 2 * TT:3 * TT]

        def rstd_from_B(extra_eps, width):
            p.op("vector", lambda eng: eng.tensor_scalar_add(
                B[:, 0:width], B[:, 0:width], extra_eps), inc=(vv, 1))
            p.op("vector", lambda eng: eng.reciprocal(
                C[:, 0:width], B[:, 0:width]), inc=(vv, 1))
            p.op("scalar", lambda eng: eng.activation(
                C[:, 0:width], C[:, 0:width], AF.Sqrt),
                waits=[(vv, vv.count)], inc=(aa, 1))
            p.op("scalar", lambda eng: eng.activation(
                A[:, 0:width], C[:, 0:width], AF.Square), inc=(aa, 1))
            p.op("vector", lambda eng: eng.tensor_mul(
                A[:, 0:width], A[:, 0:width], B[:, 0:width]),
                waits=[(aa, aa.count)], inc=(vv, 1))
            p.op("vector", lambda eng: eng.tensor_scalar(
                A[:, 0:width], A[:, 0:width], -0.5, 1.5, ALU.mult, ALU.add),
                inc=(vv, 1))
            p.op("vector", lambda eng: eng.tensor_mul(
                C[:, 0:width], C[:, 0:width], A[:, 0:width]), inc=(vv, 1))
            return vv.count

        def ln(src_view, dst_view, eps, wait0=(), n_chunks=CD, width=TT):
            ndim = n_chunks * 128
            w0 = list(wait0) + [(vv, consts_ready)]
            for c in range(n_chunks):
                par = self._sqpar
                self._sqpar ^= 1
                wsq = [(mm, self._sq_marks[par])] if par in self._sq_marks else []
                p.op("scalar", lambda eng, c=c, par=par: eng.activation(
                    sq[:, par * TT:par * TT + width], src_view(c), AF.Square),
                    waits=wsq + (w0 if c == 0 else []), inc=(aa, 1))
                p.op("tensor", lambda eng, c=c: eng.matmul(
                    ps[PS_S0][0:1, 0:width], ones_col, src_view(c),
                    start=(c == 0), stop=(c == n_chunks - 1)),
                    waits=(w0 + self._free.get(PS_S0, [])) if c == 0 else (),
                    inc=(mm, 1))
                p.op("tensor", lambda eng, c=c, par=par: eng.matmul(
                    ps[PS_S1][0:1, 0:width], ones_col,
                    sq[:, par * TT:par * TT + width],
                    start=(c == 0), stop=(c == n_chunks - 1)),
                    waits=[(aa, aa.count)] + (self._free.get(PS_S1, []) if c == 0 else []),
                    inc=(mm, 1))
                self._sq_marks[par] = mm.count
            stats_mm = mm.count
            w = [(mm, stats_mm)]
            p.op("vector", lambda eng: eng.tensor_scalar_mul(
                A[:, 0:width], ps[PS_S0][0:1, 0:width], 1.0 / ndim),
                waits=w, inc=(vv, 1))
            self._free[PS_S0] = [(vv, vv.count)]
            p.op("vector", lambda eng: eng.tensor_scalar_mul(
                B[:, 0:width], ps[PS_S1][0:1, 0:width], 1.0 / ndim),
                waits=w, inc=(vv, 1))
            self._free[PS_S1] = [(vv, vv.count)]
            mean_done = vv.count
            p.op("tensor", lambda eng: eng.matmul(
                ps[PS_B0][:, 0:width], ones_row, A[:, 0:width],
                start=True, stop=True),
                waits=[(vv, mean_done)] + self._free.get(PS_B0, []), inc=(mm, 1))
            mbc = mm.count
            p.op("scalar", lambda eng: eng.activation(
                C[:, 0:width], A[:, 0:width], AF.Square),
                waits=[(vv, mean_done)], inc=(aa, 1))
            p.op("vector", lambda eng: eng.tensor_sub(
                B[:, 0:width], B[:, 0:width], C[:, 0:width]),
                waits=[(aa, aa.count), (mm, mbc)], inc=(vv, 1))
            rstd_done = rstd_from_B(eps, width)
            p.op("tensor", lambda eng: eng.matmul(
                ps[PS_B1][:, 0:width], ones_row, C[:, 0:width],
                start=True, stop=True),
                waits=[(vv, rstd_done)] + self._free.get(PS_B1, []), inc=(mm, 1))
            bca = mm.count
            for c in range(n_chunks):
                p.op("vector", lambda eng, c=c: eng.tensor_sub(
                    dst_view(c), src_view(c), ps[PS_B0][:, 0:width]),
                    waits=[(mm, bca)] if c == 0 else (), inc=(vv, 1))
                p.op("vector", lambda eng, c=c: eng.tensor_mul(
                    dst_view(c), dst_view(c), ps[PS_B1][:, 0:width]), inc=(vv, 1))
            self._free[PS_B0] = [(vv, vv.count)]
            self._free[PS_B1] = [(vv, vv.count)]
            return vv.count, bca

        def xv(tile):
            return lambda c: xT[:, c * T + tile * TT: c * T + (tile + 1) * TT]

        def h1v(c):
            return h1[:, c * TT:(c + 1) * TT]

        gu_marks = {}
        wd_marks = {}
        gate_free = aa.count
        up_free = vv.count
        dn_free = {}
        x1dma = []

        def load_gu(tile, h):
            gseq = tile * CH + h
            w = [(mm, gu_marks[gseq - 2])] if gseq - 2 in gu_marks else []
            buf = gseq % 2
            sem = s_gu[buf]
            p.dma_ch("sync", wg_s[:, buf * D:(buf + 1) * D].rearrange("p (c n) -> p c n", c=CD),
                     dr["wgT"][:, h * 128:(h + 1) * 128].rearrange("(c p) n -> p c n", p=128),
                     sem, waits=w)
            p.dma_ch("sync", wu_s[:, buf * D:(buf + 1) * D].rearrange("p (c n) -> p c n", c=CD),
                     dr["wuT"][:, h * 128:(h + 1) * 128].rearrange("(c p) n -> p c n", p=128),
                     sem)
            return (sem, sem.count)

        def load_wd(tile, g):
            gseq = tile * (CH // 2) + g
            w = [(mm, wd_marks[gseq - 1])] if gseq - 1 in wd_marks else []
            sem = s_wd[gseq % 2]
            for i in range(2):
                h = 2 * g + i
                p.dma_ch("sync",
                         wd_s[:, i * D:(i + 1) * D].rearrange("p (c n) -> p c n", c=CD),
                         dr["wdT"][h * 128:(h + 1) * 128, :].rearrange("p (c n) -> p c n", c=CD),
                         sem, waits=w if i == 0 else ())
            return (sem, sem.count)

        for tile in range(NT):
            h1_ready, _ = ln(xv(tile), h1v, EPS_LN,
                             wait0=[(s_in, lds0)] if tile == 0 else ())
            gu_ld = {0: load_gu(tile, 0), 1: load_gu(tile, 1)}
            wd_ld = {0: load_wd(tile, 0)}
            for h in range(CH):
                gseq = tile * CH + h
                buf = gseq % 2
                gi = h % 2
                g = h // 2
                for k in range(CD):
                    p.op("tensor", lambda eng, k=k, buf=buf: eng.matmul(
                        ps[PS_GATE][:, 0:TT],
                        wg_s[:, buf * D + k * 128: buf * D + (k + 1) * 128],
                        h1v(k), start=(k == 0), stop=(k == CD - 1)),
                        waits=[gu_ld[h], (vv, h1_ready), (aa, gate_free)]
                        if k == 0 else (), inc=(mm, 1) if k == CD - 1 else None)
                gate_mm = mm.count
                for k in range(CD):
                    p.op("tensor", lambda eng, k=k, buf=buf: eng.matmul(
                        ps[PS_UP][:, 0:TT],
                        wu_s[:, buf * D + k * 128: buf * D + (k + 1) * 128],
                        h1v(k), start=(k == 0), stop=(k == CD - 1)),
                        waits=[(vv, up_free)] if k == 0 else (),
                        inc=(mm, 1) if k == CD - 1 else None)
                gu_marks[gseq] = mm.count
                up_mm = mm.count
                if h + 2 < CH:
                    gu_ld[h + 2] = load_gu(tile, h + 2)
                p.op("scalar", lambda eng, gi=gi: eng.activation(
                    act[:, gi * TT:(gi + 1) * TT], ps[PS_GATE][:, 0:TT], AF.Silu),
                    waits=[(mm, gate_mm)], inc=(aa, 1))
                gate_free = aa.count
                p.op("vector", lambda eng, gi=gi: eng.tensor_mul(
                    act[:, gi * TT:(gi + 1) * TT], act[:, gi * TT:(gi + 1) * TT],
                    ps[PS_UP][:, 0:TT]),
                    waits=[(mm, up_mm), (aa, aa.count)], inc=(vv, 1))
                up_free = vv.count
                act_ready = vv.count
                if gi == 1:
                    gsq = tile * (CH // 2) + g
                    for o in range(CD):
                        bank = PS_D0 if o % 2 == 0 else PS_D1
                        wdn = list([(vv, dn_free[o - 2])] if o - 2 in dn_free else [])
                        if o == 0:
                            wdn += [(vv, act_ready), wd_ld[g]]
                        for i in range(2):
                            p.op("tensor", lambda eng, o=o, i=i, bank=bank: eng.matmul(
                                ps[bank][:, 0:TT],
                                wd_s[:, i * D + o * 128: i * D + (o + 1) * 128],
                                act[:, i * TT:(i + 1) * TT],
                                start=(i == 0), stop=(i == 1)),
                                waits=wdn if i == 0 else (),
                                inc=(mm, 1) if i == 1 else None)
                        dn_mm = mm.count
                        if g == 0:
                            p.op("vector", lambda eng, o=o, bank=bank: eng.tensor_copy(
                                y_acc[:, o * TT:(o + 1) * TT], ps[bank][:, 0:TT]),
                                waits=[(mm, dn_mm)], inc=(vv, 1))
                        else:
                            p.op("vector", lambda eng, o=o, bank=bank: eng.tensor_add(
                                y_acc[:, o * TT:(o + 1) * TT],
                                y_acc[:, o * TT:(o + 1) * TT], ps[bank][:, 0:TT]),
                                waits=[(mm, dn_mm)], inc=(vv, 1))
                        dn_free[o] = vv.count
                        self._free[bank] = [(vv, vv.count)]
                    wd_marks[gsq] = mm.count
                    if g + 1 < CH // 2:
                        wd_ld[g + 1] = load_wd(tile, g + 1)
            for o in range(CD):
                p.op("vector", lambda eng, o=o, tile=tile: eng.scalar_tensor_tensor(
                    xv(tile)(o), y_acc[:, o * TT:(o + 1) * TT], 0.5,
                    xv(tile)(o), ALU.mult, ALU.add), inc=(vv, 1))
            x1_done = vv.count
            p.dma_ch("gpsimd",
                     dr["x1_out"].rearrange("(c p) t -> p c t", p=128)[:, :, tile * TT:(tile + 1) * TT],
                     xT[:, :].rearrange("p (c t) -> p c t", c=CD)[:, :, tile * TT:(tile + 1) * TT],
                     s_x1, waits=[(vv, x1_done)])
            x1dma.append(s_x1.count)

        ha_ready = []
        for tile in range(NT):
            mark, _ = ln(xv(tile), xv(tile), EPS_LN, wait0=[(s_x1, x1dma[-1])])
            ha_ready.append(mark)

        stage_bufs = (wg_s, wu_s)
        weff_bufs = (wd_s, h1)
        stage_ld = {}
        stage_read = {}
        weff_done = {}
        wb_read = {}

        def load_slab(s):
            w = []
            if s - 2 in stage_read:
                w.append((aa, stage_read[s - 2]))
            if s == 0:
                w.append((vv, ha_ready[-1]))
            sem = s_st[s % 2]
            p.dma_ch("sync",
                     stage_bufs[s % 2][:, 0:CD * SLAB].rearrange("p (c n) -> p c n", c=CD),
                     dr["wqkvT"][:, s * SLAB:(s + 1) * SLAB].rearrange("(c p) n -> p c n", p=128),
                     sem, waits=w)
            stage_ld[s] = (sem, sem.count)

        p.dma_ch("sync", sel16, dr["sel16_in"], s_in,
                 waits=[(s_x1, x1dma[-1])])
        sel16_ld = s_in.count
        load_slab(0)
        load_slab(1)

        for s in range(NSLAB):
            par = s % 2
            stage = stage_bufs[par]
            weff = weff_bufs[par]
            w = [stage_ld[s]]
            if s - 1 in wb_read:
                w.append((vv, wb_read[s - 1]))
            if s == 0:
                w.append((s_x1, x1dma[-1]))
            p.op("scalar", lambda eng, stage=stage: eng.copy(
                wbb[:, :], stage[:, 0:CD * SLAB]), waits=w, inc=(aa, 1))
            p.op("scalar", lambda eng, stage=stage: eng.activation(
                wab[:, :], stage[:, 0:CD * SLAB], AF.Abs), inc=(aa, 1))
            stage_read[s] = aa.count
            if s + 2 < NSLAB:
                load_slab(s + 2)
            if getattr(self, "debug_weff", False):
                p.op("vector", lambda eng, stage=stage: eng.tensor_copy(
                    y_acc[:, 7936:8192], stage[:, 0:256]), waits=[(aa, aa.count)], inc=(vv, 1))
                p.dma_ch("gpsimd", dr["dbg_wb"][s][:, 0:256], y_acc[:, 7936:8192],
                         s_dbg, waits=[(vv, vv.count)])
                self._weff_dump[s * 100 + 50] = s_dbg.count
            wab_ready = vv.count
            for c in range(CD):
                p.op("tensor", lambda eng, c=c: eng.matmul(
                    ps[PS_S0][0:16, 0:SLAB], selcol[:, (CD - 1 - c):(2 * CD - 1 - c)],
                    wab[:, c * SLAB:(c + 1) * SLAB],
                    start=(c == 0), stop=(c == CD - 1)),
                    waits=([(aa, stage_read[s])] + self._free.get(PS_S0, []))
                    if c == 0 else (), inc=(mm, 1))
            gsum_mm = mm.count
            p.op("vector", lambda eng: eng.tensor_scalar_mul(
                scl16, ps[PS_S0][0:16, 0:SLAB], 1.0 / 128.0),
                waits=[(mm, gsum_mm)], inc=(vv, 1))
            self._free[PS_S0] = [(vv, vv.count)]
            p.op("vector", lambda eng: eng.tensor_copy(sclb, scl16), inc=(vv, 1))
            p.op("vector", lambda eng: eng.tensor_copy(scf16, sclb), inc=(vv, 1))
            p.op("vector", lambda eng: eng.tensor_scalar_max(
                scf16, scf16, 1e-8), inc=(vv, 1))
            p.op("vector", lambda eng: eng.reciprocal(inv16, scf16), inc=(vv, 1))
            scl_done = vv.count
            for c in range(CD):
                p.op("tensor", lambda eng, c=c: eng.matmul(
                    ps[PS_B0][:, 0:SLAB], sel16[:, c * 128:(c + 1) * 128],
                    inv16, start=True, stop=True),
                    waits=([(vv, scl_done), (s_in, sel16_ld)] if c == 0 else [])
                    + self._free.get(PS_B0, []), inc=(mm, 1))
                p.op("tensor", lambda eng, c=c: eng.matmul(
                    ps[PS_B1][:, 0:SLAB], sel16[:, c * 128:(c + 1) * 128],
                    scf16, start=True, stop=True),
                    waits=self._free.get(PS_B1, []), inc=(mm, 1))
                bc = mm.count
                p.op("vector", lambda eng, c=c: eng.tensor_mul(
                    rb, wbb[:, c * SLAB:(c + 1) * SLAB], ps[PS_B0][:, 0:SLAB]),
                    waits=[(mm, bc)], inc=(vv, 1))
                self._free[PS_B0] = [(vv, vv.count)]
                p.op("vector", lambda eng: eng.tensor_single_scalar(
                    g1, rb, 0.5, ALU.is_gt), inc=(vv, 1))
                p.op("vector", lambda eng: eng.tensor_single_scalar(
                    g2, rb, -0.5, ALU.is_lt), inc=(vv, 1))
                p.op("vector", lambda eng: eng.tensor_sub(g1, g1, g2), inc=(vv, 1))
                wwf = []
                if c == 0 and s - 2 in weff_done:
                    wwf.append((mm, weff_done[s - 2]))
                    if self._weff_dump:
                        wwf.append((s_dbg, max(self._weff_dump.values())))
                p.op("vector", lambda eng, c=c, weff=weff: eng.tensor_mul(
                    weff[:, c * SLAB:(c + 1) * SLAB], g1, ps[PS_B1][:, 0:SLAB]),
                    waits=wwf, inc=(vv, 1))
                self._free[PS_B1] = [(vv, vv.count)]
            wb_read[s] = vv.count
            weff_ready = vv.count
            if getattr(self, "debug_weff", False):
                p.dma_ch("gpsimd", dr["dbg_weff"][s], weff[:, 0:CD * SLAB],
                         s_dbg, waits=[(vv, weff_ready)])
                self._weff_dump[s] = s_dbg.count

            for tile in range(NT):
                for oc in range(2):
                    bank = PS_GATE if oc == 0 else PS_UP
                    og = s * SLAB // 128 + oc
                    for k in range(CD):
                        p.op("tensor",
                             lambda eng, k=k, oc=oc, weff=weff, bank=bank, tile=tile:
                             eng.matmul(
                                 ps[bank][:, 0:TT],
                                 weff[:, k * SLAB + oc * 128: k * SLAB + oc * 128 + 128],
                                 xv(tile)(k), start=(k == 0), stop=(k == CD - 1)),
                             waits=[(vv, weff_ready), (vv, vv.count), (aa, aa.count)]
                             if k == 0 else (), inc=(mm, 1) if k == CD - 1 else None)
                    qkv_mm = mm.count
                    self._head(p, dr, og, tile, bank, qkv_mm, locals())
            weff_done[s] = mm.count

        p.op("gpsimd", lambda eng: eng.engine_nop(), waits=[(s_out, s_out.count), (s_x1, s_x1.count)])

    def _head(self, p, dr, og, tile, bank, qkv_mm, env):
        T, TT, NH, NKV = self.T, self.TT, self.NH, self.NKV
        ps = env["ps"]
        PS_D0, PS_D1, PS_S1, PS_B1 = 2, 3, 5, 7
        s_out, mm, vv, aa = (env[k] for k in ("s_out", "mm", "vv", "aa"))
        qraw, ro, sq = env["qraw"], env["ro"], env["sq"]
        cos2, sin2, rotc = env["cos2"], env["sin2"], env["rotc"]
        gain_s, eye_s = env["gain_s"], env["eye_s"]
        ones_col, ones_row = env["ones_col"], env["ones_row"]
        A, B, C = env["A"], env["B"], env["C"]
        s_in, lds0 = env["s_in"], env["lds0"]
        cosv = cos2[:, tile * TT:(tile + 1) * TT]
        sinv = sin2[:, tile * TT:(tile + 1) * TT]
        rstd_from_B = env["rstd_from_B"]

        p.op("vector", lambda eng: eng.tensor_copy(qraw, ps[bank][:, 0:TT]),
             waits=[(mm, qkv_mm)], inc=(vv, 1))
        evac = vv.count

        if og < NH:
            h, kind = og, "q"
        elif og < NH + NKV:
            h, kind = og - NH, "k"
        else:
            h, kind = og - NH - NKV, "v"

        if kind == "v":
            for st in range(TT // 128):
                p.op("tensor", lambda eng, st=st: eng.transpose(
                    ps[PS_D0][:, st * 128:(st + 1) * 128],
                    qraw[:, st * 128:(st + 1) * 128], eye_s[:, :]),
                    waits=([(vv, evac)] + self._free.get(PS_D0, []))
                    if st == 0 else (),
                    inc=(mm, 1) if st == TT // 128 - 1 else None)
            tmark = mm.count
            p.op("vector", lambda eng: eng.tensor_copy(ro, ps[PS_D0][:, 0:TT]),
                 waits=[(mm, tmark)], inc=(vv, 1))
            self._free[PS_D0] = [(vv, vv.count)]
            rmark = vv.count
            for st in range(TT // 128):
                p.dma_ch("gpsimd",
                         dr["v_out"][h][tile * TT + st * 128: tile * TT + (st + 1) * 128, :],
                         ro[:, st * 128:(st + 1) * 128],
                         s_out, waits=[(vv, rmark)] if st == 0 else ())
            return

        par = self._sqpar
        self._sqpar ^= 1
        wsq = [(mm, self._sq_marks[par])] if par in self._sq_marks else []
        p.op("scalar", lambda eng, par=par: eng.activation(
            sq[:, par * TT:(par + 1) * TT], qraw, AF.Square),
            waits=wsq + [(vv, evac)], inc=(aa, 1))
        p.op("tensor", lambda eng, par=par: eng.matmul(
            ps[PS_S1][0:1, 0:TT], ones_col, sq[:, par * TT:(par + 1) * TT],
            start=True, stop=True),
            waits=[(aa, aa.count)] + self._free.get(PS_S1, []), inc=(mm, 1))
        self._sq_marks[par] = mm.count
        smm = mm.count
        p.op("tensor", lambda eng: eng.matmul(
            ps[PS_D1][:, 0:TT], rotc[:, :], qraw, start=True, stop=True),
            waits=[(s_in, lds0), (vv, evac)] + self._free.get(PS_D1, []), inc=(mm, 1))
        rxm = mm.count
        p.op("vector", lambda eng: eng.tensor_scalar_mul(
            B, ps[PS_S1][0:1, 0:TT], 1.0 / 128.0),
            waits=[(mm, smm)], inc=(vv, 1))
        self._free[PS_S1] = [(vv, vv.count)]
        rstd_from_B(EPS_RMS, TT)
        if kind == "q":
            p.op("vector", lambda eng: eng.tensor_scalar_mul(
                C, C, RSQRT_HD), inc=(vv, 1))
            p.op("vector", lambda eng, h=h: eng.tensor_scalar_mul(
                C, C, gain_s[0:1, h:h + 1]), inc=(vv, 1))
        rstd_done = vv.count
        p.op("tensor", lambda eng: eng.matmul(
            ps[PS_B1][:, 0:TT], ones_row, C, start=True, stop=True),
            waits=[(vv, rstd_done)] + self._free.get(PS_B1, []), inc=(mm, 1))
        bmark = mm.count
        p.op("vector", lambda eng: eng.tensor_mul(ro, qraw, cosv), inc=(vv, 1))
        p.op("vector", lambda eng: eng.tensor_mul(qraw, ps[PS_D1][:, 0:TT], sinv),
             waits=[(mm, rxm)], inc=(vv, 1))
        self._free[PS_D1] = [(vv, vv.count)]
        p.op("vector", lambda eng: eng.tensor_add(ro, ro, qraw), inc=(vv, 1))
        p.op("vector", lambda eng: eng.tensor_mul(ro, ro, ps[PS_B1][:, 0:TT]),
             waits=[(mm, bmark)], inc=(vv, 1))
        self._free[PS_B1] = [(vv, vv.count)]
        omark = vv.count
        out = dr["q_out"] if kind == "q" else dr["k_out"]
        p.dma_ch("gpsimd", out[h][:, tile * TT:(tile + 1) * TT], ro,
                 s_out, waits=[(vv, omark)])



class L2:
    def __init__(self, S=8192, NHL=2):
        self.S = S
        self.NHL = NHL

    def build(self):
        S, NHL = self.S, self.NHL
        NQB = S // 512
        NC = S // 128
        nc = bass.Bass("TRN2", target_bir_lowering=False, debug=False,
                       detect_race_conditions=False)
        dr = {}
        dr["q2"] = nc.dram_tensor("q2", [NHL, 128, S], F32, kind="ExternalInput").ap()
        dr["k2"] = nc.dram_tensor("k2", [128, S], F32, kind="ExternalInput").ap()
        dr["v2"] = nc.dram_tensor("v2", [S, 128], F32, kind="ExternalInput").ap()
        dr["y2"] = nc.dram_tensor("y2", [NHL, 128, S], F32, kind="ExternalOutput").ap()

        with contextlib.ExitStack() as es:
            p = Prog(nc, es)
            q2 = p.sbuf("q2s", [128, NHL * S], F32)
            k2 = p.sbuf("k2s", [128, S], F32)
            v2 = p.sbuf("v2s", [128, S], F32)
            ee = p.sbuf("ee", [128, 2 * 512], F32)
            bc = p.sbuf("bc", [128, 512], F32)
            av = p.sbuf("av", [128, 512], F32)
            rr = p.sbuf("rr", [1, 512], F32)
            consts = p.sbuf("c2", [128, 132], F32)
            ps = [p.psum(f"ps{i}", [128, 512], F32) for i in range(8)]
            PS_AV, PS_R, PS_S0, PS_S1, PS_B = 0, 1, 2, 3, 4

            s_in = p.sem("s_in")
            s_out = p.sem("s_out")
            mm = p.sem("mms")
            vv = p.sem("vvs")
            aa = p.sem("aas")
            gg = p.sem("ggs")

            p.dma_ch("sync", q2[:, :].rearrange("p (h s) -> p h s", h=NHL), dr["q2"].rearrange("h p s -> p h s"), s_in)
            p.dma_ch("sync", k2[:, :], dr["k2"], s_in)
            p.dma_ch("sync", v2[:, :].rearrange("p (c n) -> p c n", c=NC),
                     dr["v2"].rearrange("(c p) n -> p c n", p=128), s_in)
            lds = s_in.count
            p.op("vector", lambda eng: eng.memset(consts[:, 0:1], 1.0), inc=(vv, 1))
            p.op("vector", lambda eng: eng.memset(consts[0:1, 1:129], 1.0), inc=(vv, 1))
            cready = vv.count
            ones_col = consts[:, 0:1]
            ones_row = consts[0:1, 1:129]

            ee_free = {}
            av_free = []
            sc_free = {}
            for qb in range(NQB):
                for h in range(NHL):
                    njc = 4 * qb + 4
                    qv = q2[:, h * S + qb * 512: h * S + (qb + 1) * 512]
                    for j in range(njc):
                        sb = PS_S0 if j % 2 == 0 else PS_S1
                        w0 = [(s_in, lds), (vv, cready)] if (qb == 0 and h == 0 and j < 2) else []
                        p.op("tensor", lambda eng, j=j, sb=sb, qv=qv: eng.matmul(
                            ps[sb][:, 0:512], k2[:, j * 128:(j + 1) * 128], qv,
                            start=True, stop=True),
                            waits=w0 + sc_free.get(sb, []), inc=(mm, 1))
                        smm = mm.count
                        par = j % 2
                        ev = ee[:, par * 512:(par + 1) * 512]
                        p.op("scalar", lambda eng, ev=ev, sb=sb: eng.activation(
                            ev, ps[sb][:, 0:512], AF.Exp),
                            waits=[(mm, smm)] + ee_free.get(par, []), inc=(aa, 1))
                        emark = aa.count
                        d = j - 4 * qb
                        if d >= 0:
                            p.op("gpsimd", lambda eng, ev=ev, d=d: eng.affine_select(
                                ev, ev, [[1, 512]], ALU.is_ge, 0.0,
                                base=-128 * d, channel_multiplier=-1),
                                waits=[(aa, emark)], inc=(gg, 1))
                            emark = None
                        ewait = [(gg, gg.count)] if emark is None else [(aa, emark)]
                        p.op("tensor", lambda eng, ev=ev, j=j, njc=njc: eng.matmul(
                            ps[PS_R][0:1, 0:512], ones_col, ev,
                            start=(j == 0), stop=(j == njc - 1)),
                            waits=ewait + (av_free if j == 0 else []), inc=(mm, 1))
                        p.op("tensor", lambda eng, ev=ev, j=j, njc=njc: eng.matmul(
                            ps[PS_AV][:, 0:512], v2[:, j * 128:(j + 1) * 128], ev,
                            start=(j == 0), stop=(j == njc - 1)), inc=(mm, 1))
                        sc_free[sb] = [(aa, aa.count)]
                        ee_free[par] = [(mm, mm.count)]
                    avm = mm.count
                    p.op("vector", lambda eng: eng.reciprocal(rr[:, :], ps[PS_R][0:1, 0:512]),
                         waits=[(mm, avm)], inc=(vv, 1))
                    rm = vv.count
                    p.op("tensor", lambda eng: eng.matmul(
                        ps[PS_B][:, 0:512], ones_row, rr[:, :], start=True, stop=True),
                        waits=[(vv, rm)], inc=(mm, 1))
                    p.op("vector", lambda eng: eng.tensor_copy(bc[:, :], ps[PS_B][:, 0:512]),
                         waits=[(mm, mm.count)], inc=(vv, 1))
                    p.op("vector", lambda eng: eng.tensor_mul(av[:, :], ps[PS_AV][:, 0:512], bc[:, :]),
                         inc=(vv, 1))
                    av_free = [(vv, vv.count)]
                    p.dma_ch("gpsimd", dr["y2"][h][:, qb * 512:(qb + 1) * 512], av[:, :],
                             s_out, waits=[(vv, vv.count)])
                    av_dma = s_out.count
                    av_free = av_free + [(s_out, av_dma)]
            p.op("gpsimd", lambda eng: eng.engine_nop(), waits=[(s_out, s_out.count)])
            p.build()
        return nc



class L3(L1):
    def build(self):
        T, TT, D, H = self.T, self.TT, self.D, self.H
        CD = self.CD
        nc = bass.Bass("TRN2", target_bir_lowering=False, debug=False,
                       detect_race_conditions=False)
        dr = {}
        dr["x_fm"] = nc.dram_tensor("x_fm", [D, T], F32, kind="ExternalInput").ap()
        dr["y_fm"] = nc.dram_tensor("y_fm", [D, T], F32, kind="ExternalInput").ap()
        dr["wgT"] = nc.dram_tensor("wgT", [D, H], F32, kind="ExternalInput").ap()
        dr["wuT"] = nc.dram_tensor("wuT", [D, H], F32, kind="ExternalInput").ap()
        dr["wdT"] = nc.dram_tensor("wdT", [H, D], F32, kind="ExternalInput").ap()
        dr["wpT"] = nc.dram_tensor("wpT", [D, D], F32, kind="ExternalInput").ap()
        dr["sel16_in"] = nc.dram_tensor("sel16_in", [16, 16 * 128], F32, kind="ExternalInput").ap()
        dr["selc_in"] = nc.dram_tensor("selc_in", [128, 2 * CD - 1], BF16, kind="ExternalInput").ap()
        dr["x3_out"] = nc.dram_tensor("x3_out", [D, T], F32, kind="ExternalOutput").ap()
        with contextlib.ExitStack() as es:
            p = Prog(nc, es)
            self._body3(nc, p, dr)
            p.build()
        return nc

    def _body3(self, nc, p, dr):
        T, TT, D, H = self.T, self.TT, self.D, self.H
        CD, CH, NT = self.CD, self.CH, self.NT
        SLAB = 256
        NSLAB = D // SLAB

        xT = p.sbuf("xT", [128, CD * T], F32)
        h1 = p.sbuf("h1", [128, CD * TT], F32)
        act = p.sbuf("act", [128, 2 * TT], F32)
        y_acc = p.sbuf("y_acc", [128, CD * TT], F32)
        wg_s = p.sbuf("wg_s", [128, 2 * D], F32)
        wu_s = p.sbuf("wu_s", [128, 2 * D], F32)
        wd_s = p.sbuf("wd_s", [128, 2 * D], F32)
        sq = p.sbuf("sq", [128, 2 * TT], F32)
        stats = p.sbuf("stats", [128, 3 * TT], F32)
        consts = p.sbuf("consts", [128, 144], F32)
        selcol = p.sbuf("selcol", [128, 2 * CD - 1], BF16)
        scl_s = p.sbuf("scl_s", [16, 896], F32)
        scl16 = scl_s[:, 0:256]
        scf16 = scl_s[:, 256:512]
        inv16 = scl_s[:, 512:768]
        sclb = scl_s[:, 768:896].bitcast(BF16)
        sel16 = p.sbuf("sel16", [16, 16 * 128], F32)
        weff_bufs = (wd_s, y_acc[:, 0:4096])
        wbb = y_acc[:, 4096:6144].bitcast(BF16)
        wab = y_acc[:, 6144:8192].bitcast(BF16)
        qsc = p.sbuf("qsc", [128, 640], F32)
        rb = qsc[:, 0:128].bitcast(BF16)
        g1 = qsc[:, 128:384]
        g2 = qsc[:, 384:640]

        ps = [p.psum(f"ps{i}", [128, 512], F32) for i in range(8)]
        PS_GATE, PS_UP, PS_D0, PS_D1, PS_S0, PS_S1, PS_B0, PS_B1 = range(8)

        s_in = p.sem("s_in")
        s_gu = [p.sem("s_gu0"), p.sem("s_gu1")]
        s_wd = [p.sem("s_wd0"), p.sem("s_wd1")]
        s_st = [p.sem("s_st0"), p.sem("s_st1")]
        s_y = p.sem("s_y")
        s_x1 = p.sem("s_x1")
        mm = p.sem("mms")
        vv = p.sem("vvs")
        aa = p.sem("aas")

        self._sqpar = 0
        self._sq_marks = {}
        self._free = {}
        self._weff_dump = {}

        p.dma_ch("sync", xT[:, :].rearrange("p (c t) -> p c t", c=CD),
                 dr["x_fm"].rearrange("(c p) t -> p c t", p=128), s_in)
        p.dma_ch("sync", selcol[:, :], dr["selc_in"], s_in)
        p.dma_ch("sync", sel16[:, :], dr["sel16_in"], s_in)
        lds0 = s_in.count
        p.op("vector", lambda eng: eng.memset(consts[:, 0:1], 1.0), inc=(vv, 1))
        p.op("vector", lambda eng: eng.memset(consts[0:1, 1:129], 1.0), inc=(vv, 1))
        consts_ready = vv.count
        ones_col = consts[:, 0:1]
        ones_row = consts[0:1, 1:129]
        A = stats[0:1, 0:TT]
        B = stats[0:1, TT:2 * TT]
        C = stats[0:1, 2 * TT:3 * TT]

        def rstd_from_B(extra_eps, width):
            p.op("vector", lambda eng: eng.tensor_scalar_add(
                B[:, 0:width], B[:, 0:width], extra_eps), inc=(vv, 1))
            p.op("vector", lambda eng: eng.reciprocal(
                C[:, 0:width], B[:, 0:width]), inc=(vv, 1))
            p.op("scalar", lambda eng: eng.activation(
                C[:, 0:width], C[:, 0:width], AF.Sqrt),
                waits=[(vv, vv.count)], inc=(aa, 1))
            p.op("scalar", lambda eng: eng.activation(
                A[:, 0:width], C[:, 0:width], AF.Square), inc=(aa, 1))
            p.op("vector", lambda eng: eng.tensor_mul(
                A[:, 0:width], A[:, 0:width], B[:, 0:width]),
                waits=[(aa, aa.count)], inc=(vv, 1))
            p.op("vector", lambda eng: eng.tensor_scalar(
                A[:, 0:width], A[:, 0:width], -0.5, 1.5, ALU.mult, ALU.add),
                inc=(vv, 1))
            p.op("vector", lambda eng: eng.tensor_mul(
                C[:, 0:width], C[:, 0:width], A[:, 0:width]), inc=(vv, 1))
            return vv.count

        def ln(src_view, dst_view, eps, wait0=(), n_chunks=CD, width=TT):
            w0 = list(wait0) + [(vv, consts_ready)]
            ndim = n_chunks * 128
            for c in range(n_chunks):
                par = self._sqpar
                self._sqpar ^= 1
                wsq = [(mm, self._sq_marks[par])] if par in self._sq_marks else []
                p.op("scalar", lambda eng, c=c, par=par: eng.activation(
                    sq[:, par * TT:par * TT + width], src_view(c), AF.Square),
                    waits=wsq + (w0 if c == 0 else []), inc=(aa, 1))
                p.op("tensor", lambda eng, c=c: eng.matmul(
                    ps[PS_S0][0:1, 0:width], ones_col, src_view(c),
                    start=(c == 0), stop=(c == n_chunks - 1)),
                    waits=(w0 + self._free.get(PS_S0, [])) if c == 0 else (),
                    inc=(mm, 1))
                p.op("tensor", lambda eng, c=c, par=par: eng.matmul(
                    ps[PS_S1][0:1, 0:width], ones_col,
                    sq[:, par * TT:par * TT + width],
                    start=(c == 0), stop=(c == n_chunks - 1)),
                    waits=[(aa, aa.count)] + (self._free.get(PS_S1, []) if c == 0 else []),
                    inc=(mm, 1))
                self._sq_marks[par] = mm.count
            stats_mm = mm.count
            w = [(mm, stats_mm)]
            p.op("vector", lambda eng: eng.tensor_scalar_mul(
                A[:, 0:width], ps[PS_S0][0:1, 0:width], 1.0 / ndim),
                waits=w, inc=(vv, 1))
            self._free[PS_S0] = [(vv, vv.count)]
            p.op("vector", lambda eng: eng.tensor_scalar_mul(
                B[:, 0:width], ps[PS_S1][0:1, 0:width], 1.0 / ndim),
                waits=w, inc=(vv, 1))
            self._free[PS_S1] = [(vv, vv.count)]
            mean_done = vv.count
            p.op("tensor", lambda eng: eng.matmul(
                ps[PS_B0][:, 0:width], ones_row, A[:, 0:width],
                start=True, stop=True),
                waits=[(vv, mean_done)] + self._free.get(PS_B0, []), inc=(mm, 1))
            mbc = mm.count
            p.op("scalar", lambda eng: eng.activation(
                C[:, 0:width], A[:, 0:width], AF.Square),
                waits=[(vv, mean_done)], inc=(aa, 1))
            p.op("vector", lambda eng: eng.tensor_sub(
                B[:, 0:width], B[:, 0:width], C[:, 0:width]),
                waits=[(aa, aa.count), (mm, mbc)], inc=(vv, 1))
            rstd_done = rstd_from_B(eps, width)
            p.op("tensor", lambda eng: eng.matmul(
                ps[PS_B1][:, 0:width], ones_row, C[:, 0:width],
                start=True, stop=True),
                waits=[(vv, rstd_done)] + self._free.get(PS_B1, []), inc=(mm, 1))
            bca = mm.count
            for c in range(n_chunks):
                p.op("vector", lambda eng, c=c: eng.tensor_sub(
                    dst_view(c), src_view(c), ps[PS_B0][:, 0:width]),
                    waits=[(mm, bca)] if c == 0 else (), inc=(vv, 1))
                p.op("vector", lambda eng, c=c: eng.tensor_mul(
                    dst_view(c), dst_view(c), ps[PS_B1][:, 0:width]), inc=(vv, 1))
            self._free[PS_B0] = [(vv, vv.count)]
            self._free[PS_B1] = [(vv, vv.count)]
            return vv.count, bca

        def xv(tile):
            return lambda c: xT[:, c * T + tile * TT: c * T + (tile + 1) * TT]

        def h1v(c):
            return h1[:, c * TT:(c + 1) * TT]

        gu_marks = {}
        wd_marks = {}
        gate_free = aa.count
        up_free = vv.count
        dn_free = {}
        x1dma = []
        l3_stage_guard = {}
        l3_weff_guard = {}

        def load_gu(tile, h):
            gseq = tile * CH + h
            w = list([(mm, gu_marks[gseq - 2])] if gseq - 2 in gu_marks else [])
            if h < 2 and tile in l3_stage_guard:
                w += l3_stage_guard[tile]
            buf = gseq % 2
            sem = s_gu[buf]
            p.dma_ch("sync", wg_s[:, buf * D:(buf + 1) * D].rearrange("p (c n) -> p c n", c=CD),
                     dr["wgT"][:, h * 128:(h + 1) * 128].rearrange("(c p) n -> p c n", p=128),
                     sem, waits=w)
            p.dma_ch("sync", wu_s[:, buf * D:(buf + 1) * D].rearrange("p (c n) -> p c n", c=CD),
                     dr["wuT"][:, h * 128:(h + 1) * 128].rearrange("(c p) n -> p c n", p=128),
                     sem)
            return (sem, sem.count)

        def load_wd(tile, g):
            gseq = tile * (CH // 2) + g
            w = list([(mm, wd_marks[gseq - 1])] if gseq - 1 in wd_marks else [])
            if g == 0 and tile in l3_weff_guard:
                w += l3_weff_guard[tile]
            sem = s_wd[gseq % 2]
            for i in range(2):
                h = 2 * g + i
                p.dma_ch("sync",
                         wd_s[:, i * D:(i + 1) * D].rearrange("p (c n) -> p c n", c=CD),
                         dr["wdT"][h * 128:(h + 1) * 128, :].rearrange("p (c n) -> p c n", c=CD),
                         sem, waits=w if i == 0 else ())
            return (sem, sem.count)

        stage_bufs = (wg_s, wu_s)
        for tile in range(NT):
            p.dma_ch("sync", h1[:, :].rearrange("p (c t) -> p c t", c=CD),
                     dr["y_fm"].rearrange("(c p) t -> p c t", p=128)[:, :, tile * TT:(tile + 1) * TT],
                     s_y, waits=[(vv, up_free)] if tile else [])
            yld = s_y.count
            for c in range(CD):
                par = self._sqpar
                self._sqpar ^= 1
                wsq = [(mm, self._sq_marks[par])] if par in self._sq_marks else []
                p.op("scalar", lambda eng, c=c, par=par: eng.activation(
                    sq[:, par * TT:(par + 1) * TT], h1v(c), AF.Square),
                    waits=wsq + ([(s_y, yld), (vv, consts_ready)] if c == 0 else []),
                    inc=(aa, 1))
                p.op("tensor", lambda eng, c=c, par=par: eng.matmul(
                    ps[PS_S1][0:1, 0:TT], ones_col, sq[:, par * TT:(par + 1) * TT],
                    start=(c == 0), stop=(c == CD - 1)),
                    waits=[(aa, aa.count)] + (self._free.get(PS_S1, []) if c == 0 else []),
                    inc=(mm, 1))
                self._sq_marks[par] = mm.count
            smm = mm.count
            p.op("vector", lambda eng: eng.tensor_scalar_mul(
                B, ps[PS_S1][0:1, 0:TT], 1.0 / D), waits=[(mm, smm)], inc=(vv, 1))
            self._free[PS_S1] = [(vv, vv.count)]
            rstd_done = rstd_from_B(EPS_RMS, TT)
            p.op("tensor", lambda eng: eng.matmul(
                ps[PS_B1][:, 0:TT], ones_row, C, start=True, stop=True),
                waits=[(vv, rstd_done)] + self._free.get(PS_B1, []), inc=(mm, 1))
            bca = mm.count
            for c in range(CD):
                p.op("vector", lambda eng, c=c: eng.tensor_mul(
                    h1v(c), h1v(c), ps[PS_B1][:, 0:TT]),
                    waits=[(mm, bca)] if c == 0 else (), inc=(vv, 1))
            self._free[PS_B1] = [(vv, vv.count)]
            yn_ready = vv.count

            stage_ld = {}
            stage_read = {}
            weff_done = {}
            wb_read = {}

            def load_slab(s, tile=tile, stage_ld=stage_ld, stage_read=stage_read):
                w = []
                if s - 2 in stage_read:
                    w.append((aa, stage_read[s - 2]))
                elif tile == 0 and s < 2:
                    w.append((vv, consts_ready))
                sem = s_st[s % 2]
                p.dma_ch("sync",
                         stage_bufs[s % 2][:, 0:CD * SLAB].rearrange("p (c n) -> p c n", c=CD),
                         dr["wpT"][:, s * SLAB:(s + 1) * SLAB].rearrange("(c p) n -> p c n", p=128),
                         sem, waits=w)
                stage_ld[s] = (sem, sem.count)

            load_slab(0)
            load_slab(1)
            for s in range(NSLAB):
                par = s % 2
                stage = stage_bufs[par]
                weff = weff_bufs[par]
                w = [stage_ld[s]]
                if s - 1 in wb_read:
                    w.append((vv, wb_read[s - 1]))
                if s == 0 and tile == 0:
                    w.append((s_in, lds0))
                if s == 0 and tile == 1:
                    w.append((vv, yn_ready))
                p.op("scalar", lambda eng, stage=stage: eng.copy(
                    wbb[:, :], stage[:, 0:CD * SLAB]), waits=w, inc=(aa, 1))
                p.op("scalar", lambda eng, stage=stage: eng.activation(
                    wab[:, :], stage[:, 0:CD * SLAB], AF.Abs), inc=(aa, 1))
                stage_read[s] = aa.count
                if s + 2 < NSLAB:
                    load_slab(s + 2)
                for c in range(CD):
                    p.op("tensor", lambda eng, c=c: eng.matmul(
                        ps[PS_S0][0:16, 0:SLAB], selcol[:, (CD - 1 - c):(2 * CD - 1 - c)],
                        wab[:, c * SLAB:(c + 1) * SLAB],
                        start=(c == 0), stop=(c == CD - 1)),
                        waits=([(aa, stage_read[s])] + self._free.get(PS_S0, []))
                        if c == 0 else (), inc=(mm, 1))
                gsum_mm = mm.count
                p.op("vector", lambda eng: eng.tensor_scalar_mul(
                    scl16, ps[PS_S0][0:16, 0:SLAB], 1.0 / 128.0),
                    waits=[(mm, gsum_mm)], inc=(vv, 1))
                self._free[PS_S0] = [(vv, vv.count)]
                p.op("vector", lambda eng: eng.tensor_copy(sclb, scl16), inc=(vv, 1))
                p.op("vector", lambda eng: eng.tensor_copy(scf16, sclb), inc=(vv, 1))
                p.op("vector", lambda eng: eng.tensor_scalar_max(
                    scf16, scf16, 1e-8), inc=(vv, 1))
                p.op("vector", lambda eng: eng.reciprocal(inv16, scf16), inc=(vv, 1))
                scl_done = vv.count
                for c in range(CD):
                    p.op("tensor", lambda eng, c=c: eng.matmul(
                        ps[PS_B0][:, 0:SLAB], sel16[:, c * 128:(c + 1) * 128],
                        inv16, start=True, stop=True),
                        waits=([(vv, scl_done)] if c == 0 else [])
                        + self._free.get(PS_B0, []), inc=(mm, 1))
                    p.op("tensor", lambda eng, c=c: eng.matmul(
                        ps[PS_B1][:, 0:SLAB], sel16[:, c * 128:(c + 1) * 128],
                        scf16, start=True, stop=True),
                        waits=self._free.get(PS_B1, []), inc=(mm, 1))
                    bc = mm.count
                    p.op("vector", lambda eng, c=c: eng.tensor_mul(
                        rb, wbb[:, c * SLAB:(c + 1) * SLAB], ps[PS_B0][:, 0:SLAB]),
                        waits=[(mm, bc)], inc=(vv, 1))
                    self._free[PS_B0] = [(vv, vv.count)]
                    p.op("vector", lambda eng: eng.tensor_single_scalar(
                        g1, rb, 0.5, ALU.is_gt), inc=(vv, 1))
                    p.op("vector", lambda eng: eng.tensor_single_scalar(
                        g2, rb, -0.5, ALU.is_lt), inc=(vv, 1))
                    p.op("vector", lambda eng: eng.tensor_sub(g1, g1, g2), inc=(vv, 1))
                    wwf = [(mm, weff_done[s - 2])] if (c == 0 and s - 2 in weff_done) else []
                    p.op("vector", lambda eng, c=c, weff=weff: eng.tensor_mul(
                        weff[:, c * SLAB:(c + 1) * SLAB], g1, ps[PS_B1][:, 0:SLAB]),
                        waits=wwf, inc=(vv, 1))
                    self._free[PS_B1] = [(vv, vv.count)]
                wb_read[s] = vv.count
                weff_ready = vv.count
                for oc in range(2):
                    bank = PS_GATE if oc == 0 else PS_UP
                    og = s * 2 + oc
                    for k in range(CD):
                        p.op("tensor",
                             lambda eng, k=k, oc=oc, weff=weff, bank=bank:
                             eng.matmul(
                                 ps[bank][:, 0:TT],
                                 weff[:, k * SLAB + oc * 128: k * SLAB + oc * 128 + 128],
                                 h1v(k), start=(k == 0), stop=(k == CD - 1)),
                             waits=[(vv, weff_ready), (vv, vv.count), (vv, yn_ready),
                                    (aa, gate_free)]
                             if k == 0 else (), inc=(mm, 1) if k == CD - 1 else None)
                    pmm = mm.count
                    p.op("vector", lambda eng, og=og, tile=tile, bank=bank:
                         eng.tensor_add(xv(tile)(og), xv(tile)(og), ps[bank][:, 0:TT]),
                         waits=[(mm, pmm)], inc=(vv, 1))
                weff_done[s] = mm.count
            l3_stage_guard[tile] = [(aa, stage_read[NSLAB - 1]),
                                    (aa, stage_read[NSLAB - 2])]
            l3_weff_guard[tile] = [(mm, weff_done[NSLAB - 1]),
                                   (mm, weff_done[NSLAB - 2])]

            h1_ready, _ = ln(xv(tile), h1v, EPS_LN)
            gu_ld = {0: load_gu(tile, 0), 1: load_gu(tile, 1)}
            wd_ld = {0: load_wd(tile, 0)}
            for h in range(CH):
                gseq = tile * CH + h
                buf = gseq % 2
                gi = h % 2
                g = h // 2
                for k in range(CD):
                    p.op("tensor", lambda eng, k=k, buf=buf: eng.matmul(
                        ps[PS_GATE][:, 0:TT],
                        wg_s[:, buf * D + k * 128: buf * D + (k + 1) * 128],
                        h1v(k), start=(k == 0), stop=(k == CD - 1)),
                        waits=[gu_ld[h], (vv, h1_ready), (aa, gate_free)]
                        if k == 0 else (), inc=(mm, 1) if k == CD - 1 else None)
                gate_mm = mm.count
                for k in range(CD):
                    p.op("tensor", lambda eng, k=k, buf=buf: eng.matmul(
                        ps[PS_UP][:, 0:TT],
                        wu_s[:, buf * D + k * 128: buf * D + (k + 1) * 128],
                        h1v(k), start=(k == 0), stop=(k == CD - 1)),
                        waits=[(vv, up_free)] if k == 0 else (),
                        inc=(mm, 1) if k == CD - 1 else None)
                gu_marks[gseq] = mm.count
                up_mm = mm.count
                if h + 2 < CH:
                    gu_ld[h + 2] = load_gu(tile, h + 2)
                p.op("scalar", lambda eng, gi=gi: eng.activation(
                    act[:, gi * TT:(gi + 1) * TT], ps[PS_GATE][:, 0:TT], AF.Silu),
                    waits=[(mm, gate_mm)], inc=(aa, 1))
                gate_free = aa.count
                p.op("vector", lambda eng, gi=gi: eng.tensor_mul(
                    act[:, gi * TT:(gi + 1) * TT], act[:, gi * TT:(gi + 1) * TT],
                    ps[PS_UP][:, 0:TT]),
                    waits=[(mm, up_mm), (aa, aa.count)], inc=(vv, 1))
                up_free = vv.count
                act_ready = vv.count
                if gi == 1:
                    gsq = tile * (CH // 2) + g
                    for o in range(CD):
                        bank = PS_D0 if o % 2 == 0 else PS_D1
                        wdn = list([(vv, dn_free[o - 2])] if o - 2 in dn_free else [])
                        if o == 0:
                            wdn += [(vv, act_ready), wd_ld[g]]
                        for i in range(2):
                            p.op("tensor", lambda eng, o=o, i=i, bank=bank: eng.matmul(
                                ps[bank][:, 0:TT],
                                wd_s[:, i * D + o * 128: i * D + (o + 1) * 128],
                                act[:, i * TT:(i + 1) * TT],
                                start=(i == 0), stop=(i == 1)),
                                waits=wdn if i == 0 else (),
                                inc=(mm, 1) if i == 1 else None)
                        dn_mm = mm.count
                        if g == 0:
                            p.op("vector", lambda eng, o=o, bank=bank: eng.tensor_copy(
                                y_acc[:, o * TT:(o + 1) * TT], ps[bank][:, 0:TT]),
                                waits=[(mm, dn_mm)], inc=(vv, 1))
                        else:
                            p.op("vector", lambda eng, o=o, bank=bank: eng.tensor_add(
                                y_acc[:, o * TT:(o + 1) * TT],
                                y_acc[:, o * TT:(o + 1) * TT], ps[bank][:, 0:TT]),
                                waits=[(mm, dn_mm)], inc=(vv, 1))
                        dn_free[o] = vv.count
                        self._free[bank] = [(vv, vv.count)]
                    wd_marks[gsq] = mm.count
                    if g + 1 < CH // 2:
                        wd_ld[g + 1] = load_wd(tile, g + 1)
            for o in range(CD):
                p.op("vector", lambda eng, o=o, tile=tile: eng.scalar_tensor_tensor(
                    xv(tile)(o), y_acc[:, o * TT:(o + 1) * TT], 0.5,
                    xv(tile)(o), ALU.mult, ALU.add), inc=(vv, 1))
            x3_done = vv.count
            p.dma_ch("gpsimd",
                     dr["x3_out"].rearrange("(c p) t -> p c t", p=128)[:, :, tile * TT:(tile + 1) * TT],
                     xT[:, :].rearrange("p (c t) -> p c t", c=CD)[:, :, tile * TT:(tile + 1) * TT],
                     s_x1, waits=[(vv, x3_done)])
            x1dma.append(s_x1.count)
        p.op("gpsimd", lambda eng: eng.engine_nop(), waits=[(s_x1, s_x1.count)])



NCORE = 8
SEQ = 8192
DM = 2048
TCORE = SEQ // NCORE


def _consts():
    eye = np.eye(128, dtype=np.float32)
    sel16 = np.zeros((16, 16 * 128), dtype=np.float32)
    for c in range(16):
        sel16[c, c * 128:(c + 1) * 128] = 1.0
    CD = DM // 128
    import ml_dtypes
    selc = np.zeros((128, 2 * CD - 1), dtype=ml_dtypes.bfloat16)
    for pp in range(128):
        selc[pp, CD - 1] = 1.0
    R = np.zeros((128, 128), dtype=np.float32)
    for i in range(64):
        R[i, 64 + i] = 1.0
        R[64 + i, i] = -1.0
    rotc = np.ascontiguousarray(R.T)
    return eye, sel16, selc, rotc


_ROPE_CACHE = {}


def _rope_tables_full():
    if "t" not in _ROPE_CACHE:
        import jax
        import jax.numpy as jnp
        cpu = jax.devices("cpu")[0]
        with jax.default_device(cpu):
            inv = 1.0 / 10000.0 ** (jnp.arange(0, 128, 2, dtype=jnp.float32) / 128)
            t = jnp.arange(SEQ, dtype=jnp.float32)
            fr = jnp.outer(t, inv)
            c = np.asarray(jnp.cos(fr)).T.astype(np.float32)
            s = np.asarray(jnp.sin(fr)).T.astype(np.float32)
        _ROPE_CACHE["t"] = (c, s)
    return _ROPE_CACHE["t"]


def _rope_tables(pos):
    c, s = _rope_tables_full()
    cc = np.ascontiguousarray(c[:, pos])
    ss = np.ascontiguousarray(s[:, pos])
    return np.concatenate([cc, cc]), np.concatenate([ss, ss])


def kernel(L0_attn_x, L0_attn_w_qkv, L0_attn_w_proj, L0_attn_q_gain,
           L0_ffn_w_gate, L0_ffn_w_up, L0_ffn_w_down):
    from concourse.bass_utils import run_bass_kernel_spmd

    x = np.ascontiguousarray(np.asarray(L0_attn_x, dtype=np.float32)[0].T)
    wgT = np.ascontiguousarray(np.asarray(L0_ffn_w_gate, np.float32).T)
    wuT = np.ascontiguousarray(np.asarray(L0_ffn_w_up, np.float32).T)
    wdT = np.ascontiguousarray(np.asarray(L0_ffn_w_down, np.float32).T)
    wqkvT = np.ascontiguousarray(np.asarray(L0_attn_w_qkv, np.float32).T)
    wpT = np.ascontiguousarray(np.asarray(L0_attn_w_proj, np.float32).T)
    gain = np.asarray(L0_attn_q_gain, np.float32)[None, :]
    eye, sel16, selc, rotc = _consts()

    im1 = []
    for c in range(NCORE):
        pos = np.arange(c * TCORE, (c + 1) * TCORE)
        cos, sin = _rope_tables(pos)
        im1.append({
            "x_fm": np.ascontiguousarray(x[:, c * TCORE:(c + 1) * TCORE]),
            "wgT": wgT, "wuT": wuT, "wdT": wdT, "wqkvT": wqkvT,
            "cos_t": cos, "sin_t": sin, "gain_in": gain,
            "eye_in": eye, "sel16_in": sel16, "selc_in": selc, "rotc_in": rotc,
        })
    nc1 = L1().build()
    r1 = run_bass_kernel_spmd(nc1, im1, core_ids=list(range(NCORE))).results

    q_all = np.concatenate([r["q_out"] for r in r1], axis=2)
    k_all = np.concatenate([r["k_out"] for r in r1], axis=2)
    v_all = np.concatenate([r["v_out"] for r in r1], axis=1)

    im2 = []
    for c in range(NCORE):
        kv = c // 2
        im2.append({
            "q2": np.ascontiguousarray(q_all[2 * c:2 * c + 2]),
            "k2": np.ascontiguousarray(k_all[kv]),
            "v2": np.ascontiguousarray(v_all[kv]),
        })
    nc2 = L2().build()
    r2 = run_bass_kernel_spmd(nc2, im2, core_ids=list(range(NCORE))).results
    y_all = np.concatenate([r["y2"] for r in r2], axis=0)
    y_fm = y_all.reshape(DM, SEQ)

    im3 = []
    for c in range(NCORE):
        im3.append({
            "x_fm": r1[c]["x1_out"],
            "y_fm": np.ascontiguousarray(y_fm[:, c * TCORE:(c + 1) * TCORE]),
            "wgT": wgT, "wuT": wuT, "wdT": wdT, "wpT": wpT,
            "sel16_in": sel16, "selc_in": selc,
        })
    nc3 = L3().build()
    r3 = run_bass_kernel_spmd(nc3, im3, core_ids=list(range(NCORE))).results
    out = np.concatenate([r["x3_out"].T for r in r3], axis=0)
    return np.ascontiguousarray(out[None, :, :])
